# version 29
# baseline (speedup 1.0000x reference)
import numpy as np
import concourse.bass as bass
import concourse.mybir as mybir
from concourse.bass_utils import run_bass_kernel_spmd

F32 = mybir.dt.float32
BF16 = mybir.dt.bfloat16
FP8 = mybir.dt.float8e4
AF = mybir.ActivationFunctionType
OP = mybir.AluOpType
AX = mybir.AxisListType

C32 = float(1.5 * 2.0**23)
C64 = float(1.5 * 2.0**52)
EPS_LN = 1e-5
D = 1024
DFF = 4096
G = 64
TCORE = 4096
TT = 512
DEBUG = False
STAGE = 99
SUB = 99
NTILE = TCORE // TT
NTG = TT // 128


class Prog:

    QUEUES = ("qL", "qX", "qS", "pA", "pB", "sA", "sB",
              "gA", "gB", "hA", "hB", "wA", "wB")
    CENG = ("v", "s", "t", "g")

    def __init__(self, nc, es):
        self.nc = nc
        self.sem = {}
        for name in ("cv", "cs", "ct", "cg", "qL", "qX", "qS", "pA", "pB",
                     "sA", "sB", "gA", "gB", "hA", "hB", "wA", "wB"):
            self.sem[name] = es.enter_context(nc.semaphore(name))
        self.esem = {"v": "cv", "s": "cs", "t": "ct", "g": "cg"}
        self.ops = {e: [] for e in ("v", "s", "t", "g", "y")}
        self.nreal = {e: 0 for e in ("v", "s", "t", "g", "y")}
        self.qcnt = {q: 0 for q in self.QUEUES}
        self.entered = {}
        self.phase_id = 0
        self.snap = {}

    def n(self, eng):
        return self.nreal[eng]

    def q(self, qname):
        return self.qcnt[qname]

    def enter_phase(self):
        self.phase_id += 1
        self.snap = dict(self.qcnt)
        for e in self.CENG:
            self.snap[e] = self.nreal[e]

    def _waits(self, eng, extra):
        w = []
        if self.entered.get(eng) != self.phase_id:
            self.entered[eng] = self.phase_id
            for k, v in self.snap.items():
                if k != eng and v > 0:
                    w.append((k, v))
        if extra:
            w += extra
        return w

    def op(self, eng, emit, waits=None, small=True):
        w = self._waits(eng, waits)
        if small and eng in self.CENG and self.nreal[eng] > 0:
            w = w + [(eng, self.nreal[eng])]
        ridx = self.nreal.get(eng)
        if eng in self.CENG:
            self.nreal[eng] += 1
        self.ops[eng].append([w, emit, None, ridx])

    def mm(self, emit, waits=None, inc=False):
        self.op("t", emit, waits, small=False)

    def dma(self, eng, qn, out, in_, waits=None):
        w = self._waits(eng, waits)
        self.ops[eng].append(
            [w, lambda h, o=out, i=in_: h.dma_start(o, i), qn, None])
        self.qcnt[qn] += 1

    def dmat(self, out, in_, waits=None, qn="qX", eng="y"):
        w = self._waits(eng, waits)
        self.ops[eng].append(
            [w, lambda h, o=out, i=in_: h.dma_start_transpose(o, i), qn,
             None])
        self.qcnt[qn] += 1

    def waitonly(self, eng, waits):
        self.ops[eng].append([self._waits(eng, waits), None, None, None])

    def finalize(self):
        targets = {e: set() for e in self.CENG}
        for e in self.ops:
            for w, _, _, _ in self.ops[e]:
                for (k, v) in w:
                    if k in self.CENG and v > 0:
                        targets[k].add(v - 1)
        self.incpos = {}
        self.rank = {}
        for e in self.CENG:
            pos = sorted(targets[e])
            self.incpos[e] = set(pos)
            r = {}
            cnt = 0
            for j in pos:
                cnt += 1
                r[j] = cnt
            self.rank[e] = r

    def _wait_val(self, k, v):
        if k in self.QUEUES:
            return self.sem[k], 16 * v
        return self.sem[self.esem[k]], self.rank[k][v - 1]

    def replay(self, eng, h):
        is_ceng = eng in self.CENG
        for (waits, emit, tag, ridx) in self.ops[eng]:
            for (k, v) in waits:
                if v == 0:
                    continue
                sem, val = self._wait_val(k, v)
                h.wait_ge(sem, val)
            if emit is None:
                continue
            ins = emit(h)
            if tag in self.QUEUES:
                ins.then_inc(self.sem[tag], 16)
            elif is_ceng and ridx in self.incpos[eng]:
                ins.then_inc(self.sem[self.esem[eng]], 1)


def build(nc):
    from contextlib import ExitStack
    es = ExitStack()
    sbuf = lambda n_, s_, d_=F32: es.enter_context(nc.sbuf_tensor(n_, s_, d_))

    x_in = nc.dram_tensor("x", [TCORE, D], F32, kind="ExternalInput").ap()
    xg_in = nc.dram_tensor("xg", [G, D], F32, kind="ExternalInput").ap()
    Wins = {}
    for n, r, c in (("WqT", D, D), ("WkT", D, D), ("WvT", D, D), ("WoT", D, D),
                    ("WgT", D, DFF), ("WuT", D, DFF), ("WdT", DFF, D)):
        Wins[n] = nc.dram_tensor(n, [r, c], F32, kind="ExternalInput").ap()
    cst_in = nc.dram_tensor("cst", [128, 8], F32, kind="ExternalInput").ap()
    idf_in = nc.dram_tensor("idf", [128, 128], F32, kind="ExternalInput").ap()
    y_out = nc.dram_tensor("y", [TCORE, D], F32, kind="ExternalOutput").ap()
    S_tern = [nc.dram_tensor("Stg", [D, DFF], FP8).ap(),
              nc.dram_tensor("Stu", [D, DFF], FP8).ap(),
              nc.dram_tensor("Std", [DFF, D], FP8).ap()]
    S_hq = nc.dram_tensor("Shq", [TT, DFF], BF16).ap()

    cst_sb = sbuf("cst_sb", [128, 8])
    idf_sb = sbuf("idf_sb", [128, 128])
    w_sb = sbuf("w_sb", [128, 1])
    omw_sb = sbuf("omw_sb", [128, 1])
    onesf = sbuf("onesf", [1, 128])
    onesc = sbuf("onesc", [128, 1])
    fw_sb = sbuf("fw_sb", [128, 4])
    sw_sb = sbuf("sw_sb", [128, 4])
    kg_ev = sbuf("kg_ev", [128, 8 * G], BF16)
    kg_od = sbuf("kg_od", [128, 8 * G], BF16)
    vgT = sbuf("vgT", [G, 8 * 160], BF16)
    stat = sbuf("stat", [128, 48])
    fstat = sbuf("fstat", [128, 16])
    scr = sbuf("scr", [128, D])
    scr2 = sbuf("scr2", [128, D])
    xgl = sbuf("xgl", [G, D], BF16)
    xg_ft = sbuf("xg_ft", [G, D])
    xg_f = xg_ft[0:G, :]
    xgT = sbuf("xgT", [128, 8 * G], BF16)
    vg_f = sbuf("vg_f", [128, G])

    xtA = sbuf("xtA", [128, NTG * D])
    xt = [xtA[:, i * D:(i + 1) * D] for i in range(NTG)]
    lnTA = sbuf("lnTA", [128, 8 * TT], BF16)
    lnT = [lnTA[:, i * TT:(i + 1) * TT] for i in range(8)]
    tmp_tm = sbuf("tmp_tm", [128, D], BF16)
    karena = sbuf("karena", [128, 2 * DFF], BF16)
    qarena = sbuf("qarena", [128, DFF], BF16)
    varena = sbuf("varena", [128, DFF], BF16)
    warena = sbuf("warena", [128, 2 * DFF], BF16)
    parena = sbuf("parena", [128, DFF], BF16)
    atarena = sbuf("atarena", [128, DFF], BF16)
    kev = [karena[:, c * TT:(c + 1) * TT] for c in range(8)]
    kod = [karena[:, DFF + c * TT:DFF + (c + 1) * TT] for c in range(8)]
    q_fm = [qarena[:, c * TT:(c + 1) * TT] for c in range(8)]
    v_fm = [varena[:, c * TT:(c + 1) * TT] for c in range(8)]
    ternU_v = [qarena[:, :], varena[:, :]]
    wbuf = [warena[:, r * D:(r + 1) * D] for r in range(8)]
    ternG_v = [warena[:, 0:DFF], warena[:, DFF:2 * DFF]]
    probsT_l = parena[:, 0:16 * 128]
    probsT_g = parena[:, 16 * 128:32 * 128]
    afmT = [parena[:, ic * TT:(ic + 1) * TT] for ic in range(8)]
    attn_tm = [atarena[:, g * D:(g + 1) * D] for g in range(NTG)]
    hqtm = atarena[:, :]
    tstg = atarena[:, 0:2 * D]
    rsb = qarena[:, :]
    vTe = [sbuf(f"vTe{i}", [128, NTG * 80], BF16) for i in range(8)]
    vTo = [sbuf(f"vTo{i}", [128, NTG * 80], BF16) for i in range(8)]
    probs_l = sbuf("probs_l", [128, 16 * 128], BF16)
    pgc = sbuf("pgc", [128, 16 * 64], BF16)
    probs_g = sbuf("probs_g", [128, 16 * 128], BF16)
    attn_un = sbuf("attn_un", [128, 16 * 65])
    srec = sbuf("srec", [128, 16])
    pevac = sbuf("pevac", [128, TT])
    hbuf = [sbuf(f"hbuf{i}", [128, DFF]) for i in range(NTG)]
    devac = [hbuf[oc % 4][:, (oc // 4) * 2048:(oc // 4) * 2048 + TT]
             for oc in range(8)]
    hqT_b = sbuf("hqT_b", [128, 2 * TT], BF16)
    wdb = sbuf("wdb", [128, 2 * D], BF16)

    psA = es.enter_context(nc.psum_tensor("psA", [128, 4096], F32))

    P = Prog(nc, es)

    def v_op(emit=None, waits=None, big=False):
        P.op("v", emit, waits, small=True)

    def s_op(emit, waits=None):
        P.op("s", emit, waits, small=False)

    def ln_stats(xtile, pc):
        s0 = stat[0:pc, 0:1]; s1 = stat[0:pc, 1:2]; s2 = stat[0:pc, 2:3]
        s3 = stat[0:pc, 3:4]; s4 = stat[0:pc, 4:5]; s5 = stat[0:pc, 5:6]
        s6 = stat[0:pc, 6:7]; s7 = stat[0:pc, 7:8]; s8 = stat[0:pc, 8:9]
        s9 = stat[0:pc, 9:10]
        dump = scr[0:pc, :]
        v_op(lambda h, o=dump, x=xtile, a=s0: h.tensor_scalar(
            o, x, 1.0, 0.0, OP.mult, OP.add, accum_out=a), big=True)
        v_op(lambda h, o=dump, x=xtile, a=s1: h.scalar_tensor_tensor(
            o, x, 1.0, x, OP.mult, OP.mult, accum_out=a), big=True)
        v_op(lambda h, o=s2, i=s0: h.tensor_scalar(o, i, 1.0 / D, None, OP.mult))
        v_op(lambda h, o=s3, i=s2: h.tensor_scalar(o, i, s2, None, OP.mult))
        v_op(lambda h, o=s4, i=s1, m=s3: h.scalar_tensor_tensor(
            o, i, 1.0 / D, m, OP.mult, OP.subtract))
        v_op(lambda h, o=s4: h.tensor_scalar(o, o, EPS_LN, None, OP.add))
        s_op(lambda h, o=s5, i=s4: h.activation(o, i, AF.Sqrt),
             waits=[("v", P.n("v"))])
        v_op(lambda h, o=s6, i=s5: h.reciprocal(o, i),
             waits=[("s", P.n("s"))])
        v_op(lambda h, o=s7, i=s6: h.tensor_scalar(o, i, s6, None, OP.mult))
        v_op(lambda h, o=s8, i=s7, u=s4: h.scalar_tensor_tensor(
            o, i, -0.5, u, OP.mult, OP.mult))
        v_op(lambda h, o=s8: h.tensor_scalar(o, o, 1.5, None, OP.add))
        v_op(lambda h, o=s9, i=s6, c=s8: h.tensor_scalar(o, i, c, None, OP.mult))

    def g_op(emit=None, waits=None):
        P.op("g", emit, waits, small=True)

    def ln_accum(xtile, col):
        v_op(lambda h, o=scr[:, :], x=xtile, a=stat[:, col:col + 1]:
             h.tensor_scalar(o, x, 1.0, 0.0, OP.mult, OP.add, accum_out=a),
             big=True)
        v_op(lambda h, o=scr[:, :], x=xtile, a=stat[:, 4 + col:5 + col]:
             h.scalar_tensor_tensor(o, x, 1.0, x, OP.mult, OP.mult,
                                    accum_out=a), big=True)

    def ln_chain4():
        C = lambda a, b: stat[:, a:b]
        v_op(lambda h: h.tensor_scalar(C(8, 12), C(0, 4), 1.0 / D, None,
                                       OP.mult))
        v_op(lambda h: h.tensor_tensor(C(12, 16), C(8, 12), C(8, 12), OP.mult))
        v_op(lambda h: h.scalar_tensor_tensor(C(16, 20), C(4, 8), 1.0 / D,
                                              C(12, 16), OP.mult, OP.subtract))
        v_op(lambda h: h.tensor_scalar(C(16, 20), C(16, 20), EPS_LN, None,
                                       OP.add))
        s_op(lambda h: h.activation(C(24, 28), C(16, 20), AF.Sqrt),
             waits=[("v", P.n("v"))])
        v_op(lambda h: h.reciprocal(C(28, 32), C(24, 28)),
             waits=[("s", P.n("s"))])
        v_op(lambda h: h.tensor_tensor(C(32, 36), C(28, 32), C(28, 32),
                                       OP.mult))
        v_op(lambda h: h.scalar_tensor_tensor(C(32, 36), C(32, 36), -0.5,
                                              C(16, 20), OP.mult, OP.mult))
        v_op(lambda h: h.tensor_scalar(C(32, 36), C(32, 36), 1.5, None,
                                       OP.add))
        v_op(lambda h: h.tensor_tensor(C(20, 24), C(28, 32), C(32, 36),
                                       OP.mult))

    P.enter_phase()
    P.dma("y", "qL", cst_sb[:], cst_in)
    P.dma("y", "qL", idf_sb[:], idf_in)
    P.dma("y", "qL", xg_f[:, :], xg_in)
    ld0 = P.q("qL")

    P.waitonly("v", [("qL", ld0)])
    if SUB < 1:
        v_op(lambda h: h.memset(w_sb[:], 0.0))
    if SUB >= 1:
        v_op(lambda h: h.tensor_add(w_sb[:], cst_sb[:, 1:2], cst_sb[:, 0:1]))
    v_op(lambda h: h.tensor_sub(w_sb[:], w_sb[:], cst_sb[:, 1:2]))
    v_op(lambda h: h.tensor_scalar(w_sb[:], w_sb[:], float(2.0**30), None,
                                   OP.mult))
    v_op(lambda h: h.scalar_tensor_tensor(omw_sb[:], w_sb[:], -1.0,
                                          cst_sb[:, 1:2], OP.mult, OP.add))
    v_op(big=True, emit=lambda h: h.memset(onesf[:], 1.0))
    v_op(big=True, emit=lambda h: h.memset(onesc[:], 1.0))
    for c in (range(8) if SUB >= 2 else []):
        v_op(big=True, emit=lambda h, t=kev[c]: h.memset(t[64:128, :], 0.0))
        v_op(big=True, emit=lambda h, t=kod[c]: h.memset(t[0:64, :], 0.0))
    v_op(big=True, emit=lambda h: h.memset(kg_ev[64:128, :], 0.0))
    v_op(big=True, emit=lambda h: h.memset(kg_od[0:64, :], 0.0))
    v_op(big=True, emit=lambda h: h.memset(probs_g[:, :], 0.0))
    for c in (range(8) if SUB >= 3 else []):
        v_op(big=True, emit=lambda h, t=vTe[c]: h.memset(
            t[:].rearrange("p (w k) -> p w k", k=80)[:, :, 64:65], 1.0))
        v_op(big=True, emit=lambda h, t=vTo[c]: h.memset(
            t[:].rearrange("p (w k) -> p w k", k=80)[:, :, 64:65], 1.0))
    v_op(big=True, emit=lambda h: h.memset(
        vgT[:].rearrange("p (c k) -> p c k", k=80)[:, :, 64:65], 1.0))

    for wi, (wname, Sdst, RR, CC) in ([] if STAGE < 1 else enumerate(
            (("WgT", S_tern[0], D, DFF), ("WuT", S_tern[1], D, DFF),
             ("WdT", S_tern[2], DFF, D)))):
        Wsrc = Wins[wname]
        tq8 = ("pA", "pB", "hA", "hB", "wA", "wB", "gA", "gB")
        wide = (CC == DFF)
        if wide:
            chs = [(s, 0) for s in range(RR // 128)]
            bufs = [hbuf[g] for g in range(4)]
            nbuf = 4
        else:
            chs = [(r, cb) for r in range(RR // 128) for cb in range(CC // D)]
            bufs = [scr, scr2] + [hbuf[g][:, 1024:2048] for g in range(4)] + \
                   [hbuf[0][:, 2560:3584], hbuf[1][:, 2560:3584]]
            nbuf = 8
        P.enter_phase()
        cons = {}
        for i, (r, cb) in enumerate(chs):
            pq = tq8[i % nbuf]
            if i >= nbuf:
                P.waitonly("y", [("v", cons[i - nbuf])])
            cw = CC if wide else D
            P.dma("y", pq, bufs[i % nbuf][:, :],
                  Wsrc[r * 128:(r + 1) * 128, cb * cw:(cb + 1) * cw])
            P.waitonly("v", [(pq, P.q(pq))])
            v_op(lambda h, b=bufs[i % nbuf]: h.tensor_reduce(
                stat[:, 12:13], b[:, :], AX.X, OP.add,
                apply_absolute_value=True), big=True)
            if i == 0:
                v_op(lambda h: h.tensor_copy(stat[:, 13:14], stat[:, 12:13]))
            else:
                v_op(lambda h: h.tensor_add(stat[:, 13:14], stat[:, 13:14],
                                            stat[:, 12:13]))
            cons[i] = P.n("v")
        P.waitonly("t", [("v", P.n("v"))])
        P.mm(lambda h: h.matmul(psA[0:1, 0:1], onesc[:], stat[:, 13:14],
                                start=True, stop=True), inc=True)
        P.waitonly("v", [("t", P.n("t"))])
        nelem = float(RR * CC)
        v_op(lambda h, s=nelem: h.tensor_scalar(stat[0:1, 14:15], psA[0:1, 0:1],
                                                1.0 / s, None, OP.mult))
        v_op(lambda h: h.tensor_scalar(stat[0:1, 14:15], stat[0:1, 14:15],
                                       1e-5, None, OP.max))
        v_op(lambda h: h.reciprocal(stat[0:1, 15:16], stat[0:1, 14:15]))
        P.waitonly("t", [("v", P.n("v"))])
        P.mm(lambda h: h.matmul(psA[0:128, 512:514], onesf[:],
                                stat[0:1, 14:16], start=True, stop=True),
             inc=True)
        P.waitonly("v", [("t", P.n("t"))])
        v_op(lambda h, wi=wi: h.tensor_copy(fw_sb[:, wi:wi + 1],
                                            psA[:, 512:513]))
        v_op(lambda h, wi=wi: h.tensor_copy(sw_sb[:, wi:wi + 1],
                                            psA[:, 513:514]))
        P.enter_phase()
        cons2 = {}
        stdone = {}
        tslabs = [atarena[:, :], parena[:, :]] if wide else \
                 [tstg[:, 0:D], tstg[:, D:2 * D]]
        for i, (r, cb) in enumerate(chs):
            pq = tq8[i % nbuf]
            sq = ("sA", "sB")[i % 2]
            if i == 0:
                P.waitonly("y", [("v", P.n("v"))])
            if i >= nbuf:
                P.waitonly("y", [("v", cons2[i - nbuf])])
            cw = CC if wide else D
            P.dma("y", pq, bufs[i % nbuf][:, :],
                  Wsrc[r * 128:(r + 1) * 128, cb * cw:(cb + 1) * cw])
            b = bufs[i % nbuf]
            t = tslabs[i % 2]
            P.waitonly("v", [(pq, P.q(pq))])
            if i >= 2:
                P.waitonly("v", [(sq, stdone[i - 2])])
            v_op(lambda h, b=b, t=t, wi=wi: h.tensor_scalar(
                t[:, :], b[:, :], sw_sb[:, wi:wi + 1], 0.5, OP.mult,
                OP.is_ge), big=True)
            v_op(lambda h, b=b, wi=wi: h.tensor_scalar(
                b[:, :], b[:, :], sw_sb[:, wi:wi + 1], -0.5, OP.mult,
                OP.is_le), big=True)
            v_op(lambda h, b=b, t=t: h.tensor_sub(t[:, :], t[:, :], b[:, :]))
            cons2[i] = P.n("v")
            P.dma("g", sq, Sdst[r * 128:(r + 1) * 128, cb * cw:(cb + 1) * cw],
                  t[:, :], waits=[("v", P.n("v"))])
            stdone[i] = P.q(sq)

    P.enter_phase()
    if STAGE < 2:
        for g_ in range(NTG):
            P.dma("g", "qS", y_out[g_ * 128:(g_ + 1) * 128, :], scr[:, :])
        P.enter_phase()
        P.waitonly("g", [("qS", P.q("qS")), ("qL", P.q("qL")),
                         ("qX", P.q("qX"))])
        P.finalize()
        with es, nc.Block() as block:
            block.vector(lambda h: P.replay("v", h))
            block.scalar(lambda h: P.replay("s", h))
            block.tensor(lambda h: P.replay("t", h))
            block.gpsimd(lambda h: P.replay("g", h))
            block.sync(lambda h: P.replay("y", h))
        return nc
    ln_stats(xg_f[:, :], G)
    v_op(lambda h: h.scalar_tensor_tensor(
        stat[0:G, 10:11], stat[0:G, 2:3], -1.0, stat[0:G, 9:10],
        OP.mult, OP.mult))
    v_op(lambda h: h.tensor_scalar(xgl[:, :], xg_f[:, :], stat[0:G, 9:10],
                                   stat[0:G, 10:11], OP.mult, OP.add))
    P.dmat(xgT[:].rearrange("p (i g) -> p i g", i=8),
           xgl[:, :].rearrange("p (i k) -> p i k", i=8),
           waits=[("v", P.n("v"))])
    for widx, isK in ((1, True), (2, False)):
        P.enter_phase()
        wn = ("WqT", "WkT", "WvT", "WoT")[widx]
        for r in range(8):
            P.dma("g", "qS", wbuf[r][:, :], Wins[wn][r * 128:(r + 1) * 128, :])
        wready = P.q("qS")
        P.waitonly("t", [("qS", wready), ("qX", P.q("qX")),
                         ("v", P.n("v"))])
        evcv = {}
        for oc in range(8):
            slot = (oc % 2) * 512
            if oc >= 2:
                P.waitonly("t", [("v", evcv[oc - 2])])
            for ic in range(8):
                P.mm(lambda h, oc=oc, ic=ic, slot=slot: h.matmul(
                    psA[:, slot:slot + G],
                    wbuf[ic][:, oc * 128:(oc + 1) * 128],
                    xgT[:, ic * G:(ic + 1) * G],
                    start=(ic == 0), stop=(ic == 7)), inc=(ic == 7))
            if isK:
                v_op(lambda h, oc=oc, slot=slot: h.tensor_copy(
                    kg_ev[0:64, oc * G:(oc + 1) * G], psA[0:64, slot:slot + G]),
                    waits=[("t", P.n("t"))], big=True)
                v_op(lambda h, oc=oc, slot=slot: h.tensor_copy(
                    kg_od[64:128, oc * G:(oc + 1) * G],
                    psA[64:128, slot:slot + G]), big=True)
            else:
                v_op(lambda h, slot=slot: h.tensor_copy(
                    vg_f[:, :], psA[:, slot:slot + G]),
                    waits=[("t", P.n("t"))])
                P.mm(lambda h: h.transpose(psA[0:64, 1024:1152], vg_f[:, :],
                                           idf_sb[:]),
                     waits=[("v", P.n("v"))], inc=True)
                v_op(lambda h, oc=oc: h.tensor_copy(
                    vgT[:, oc * 160:oc * 160 + 64], psA[0:64, 1024:1088]),
                    waits=[("t", P.n("t"))])
                v_op(lambda h, oc=oc: h.tensor_copy(
                    vgT[:, oc * 160 + 80:oc * 160 + 144], psA[0:64, 1088:1152]))
            evcv[oc] = P.n("v")

    if DEBUG == 5:
        P.enter_phase()
        P.dma("g", "qS", y_out[0:128, 0:512], kg_ev[:, :],
              waits=[("v", P.n("v")), ("t", P.n("t")), ("s", P.n("s"))])
        P.dma("g", "qS", y_out[128:256, 0:512], kg_od[:, :])
        P.dma("g", "qS", y_out[256:384, 0:512], xgT[:, :])
        P.dma("g", "qS", y_out[384:448, 0:1024], xgl[:, :])
        P.dma("g", "qS", y_out[448:512, 0:1024], vgT[:, :].rearrange(
            "p (a b) -> p a b", b=160)[:, 0:6].rearrange("p a b -> p (a b)")
              if False else vgT[:, 0:1024])

    _ntile = 0 if STAGE < 3 else (1 if STAGE < 8 else NTILE)
    for ti in range(_ntile):
        t0 = ti * TT
        P.enter_phase()
        for g_ in range(NTG):
            P.dma("y", "qL", xt[g_][:, :],
                  x_in[t0 + g_ * 128:t0 + (g_ + 1) * 128, :])
        xld = P.q("qL")
        for r in range(8):
            P.dma("g", "qS", wbuf[r][:, :], Wins["WqT"][r * 128:(r + 1) * 128, :])
        wq_ready = P.q("qS")
        if ti == 0 or DEBUG or STAGE < 8:
            P.waitonly("v", [("qL", xld)])
            for g_ in range(NTG):
                ln_accum(xt[g_][:, :], g_)
        ln_chain4()
        v_op(lambda h: h.scalar_tensor_tensor(
            stat[:, 36:40], stat[:, 8:12], -1.0, stat[:, 20:24],
            OP.mult, OP.mult))
        P.waitonly("v", [("qL", xld)])
        for g_ in range(NTG):
            if g_ > 0:
                P.waitonly("v", [("qX", P.q("qX"))])
            v_op(lambda h, x=xt[g_], g_=g_: h.tensor_scalar(
                tmp_tm[:, :], x[:, :], stat[:, 20 + g_:21 + g_],
                stat[:, 36 + g_:37 + g_], OP.mult, OP.add))
            P.dmat(lnTA[:].rearrange("p (i t) -> p i t", i=8)[
                       :, :, g_ * 128:(g_ + 1) * 128],
                   tmp_tm[:].rearrange("p (i k) -> p i k", i=8),
                   waits=[("v", P.n("v"))])
        lnT_ready = P.q("qX")

        for pj in range(3):
            P.enter_phase()
            if pj > 0:
                wn = ("WqT", "WkT", "WvT")[pj]
                for r in range(8):
                    P.dma("g", "qS", wbuf[r][:, :],
                          Wins[wn][r * 128:(r + 1) * 128, :],
                          waits=[("t", P.n("t"))])
            wrdy = P.q("qS") if pj > 0 else wq_ready
            P.waitonly("t", [("qS", wrdy), ("qX", lnT_ready)])
            evcv = {}
            for oc in range(8):
                slot = (oc % 2) * 512
                if oc >= 2:
                    P.waitonly("t", [("s", evcv[oc - 2])])
                for ic in range(8):
                    P.mm(lambda h, oc=oc, ic=ic, slot=slot: h.matmul(
                        psA[:, slot:slot + 512],
                        wbuf[ic][:, oc * 128:(oc + 1) * 128],
                        lnT[ic][:, :], start=(ic == 0), stop=(ic == 7)),
                        inc=(ic == 7))
                ps = psA[:, slot:slot + 512]
                if pj == 0:
                    s_op(lambda h, o=q_fm[oc], ps=ps: h.copy(o[:, :], ps),
                         waits=[("t", P.n("t"))])
                elif pj == 1:
                    s_op(lambda h, o=kev[oc], ps=ps: h.copy(
                        o[0:64, :], ps[0:64, :]), waits=[("t", P.n("t"))])
                    s_op(lambda h, o=kod[oc], ps=ps: h.copy(
                        o[64:128, :], ps[64:128, :]))
                else:
                    s_op(lambda h, o=v_fm[oc], ps=ps: h.copy(o[:, :], ps),
                         waits=[("t", P.n("t"))])
                evcv[oc] = P.n("s")
        if DEBUG == 2 and ti == 0:
            for c in range(8):
                P.dma("g", "qS", y_out[c * 128:(c + 1) * 128, 0:512],
                      q_fm[c][:, :], waits=[("v", P.n("v"))])
                P.dma("g", "qS", y_out[1024 + c * 128:1024 + (c + 1) * 128, 0:512],
                      kev[c][:, :])
                P.dma("g", "qS", y_out[2048 + c * 128:2048 + (c + 1) * 128, 0:512],
                      kod[c][:, :])
                P.dma("g", "qS", y_out[3072 + c * 128:3072 + (c + 1) * 128, 0:512],
                      v_fm[c][:, :])
        for c in range(8):
            P.dmat(vTe[c][:].rearrange("p (w k) -> p w k", k=80)[:, :, 0:64],
                   v_fm[c][0:64, :].rearrange("p (w k) -> p w k", w=NTG),
                   waits=[("s", P.n("s"))] if c == 0 else None)
            P.dmat(vTo[c][:].rearrange("p (w k) -> p w k", k=80)[:, :, 0:64],
                   v_fm[c][64:128, :].rearrange("p (w k) -> p w k", w=NTG))
        vT_ready = P.q("qX")
        for r in range(8):
            P.dma("g", "qS", wbuf[r][:, :], Wins["WoT"][r * 128:(r + 1) * 128, :],
                  waits=[("t", P.n("t"))])
        wo_ready = P.q("qS")

        if STAGE >= 4:
            P.enter_phase()
        evac_prev = None
        av_prev = None
        for w_ in (range(NTG) if STAGE >= 4 else []):
            if w_ == 0:
                P.waitonly("t", [("qX", vT_ready), ("v", P.n("v")),
                                 ("s", P.n("s"))])
            else:
                P.waitonly("t", [("qX", vT_ready), ("v", evac_prev)])
            for s_ in range(16):
                c, p_ = s_ // 2, s_ % 2
                kj = (kev if p_ == 0 else kod)[c]
                kg = kg_ev if p_ == 0 else kg_od
                P.mm(lambda h, q=q_fm[c], kg=kg, c=c, s_=s_, w_=w_: h.matmul(
                    psA[:, 256 * s_:256 * s_ + G],
                    q[:, w_ * 128:(w_ + 1) * 128],
                    kg[:, c * G:(c + 1) * G], start=True, stop=True))
                P.mm(lambda h, q=q_fm[c], kj=kj, s_=s_, w_=w_: h.matmul(
                    psA[:, 256 * s_ + 64:256 * s_ + 192],
                    q[:, w_ * 128:(w_ + 1) * 128],
                    kj[:, w_ * 128:(w_ + 1) * 128],
                    start=True, stop=True), inc=(s_ == 15))
            sc_done = P.n("t")
            if DEBUG == 6 and ti == 0 and w_ == 0:
                v_op(lambda h: h.tensor_copy(hbuf[0][:, :], psA[:, 0:4096]),
                     waits=[("t", sc_done)], big=True)
                for qq in range(4):
                    P.dma("g", "qS", y_out[qq * 128:(qq + 1) * 128, :],
                          hbuf[0][:, qq * 1024:(qq + 1) * 1024],
                          waits=[("v", P.n("v"))])
                P.waitonly("s", [("qS", P.q("qS"))])
                P.waitonly("t", [("qS", P.q("qS"))])
                P.waitonly("v", [("qS", P.q("qS"))])
            s_op(lambda h: h.activation(
                pgc[:, :],
                psA[:].rearrange("p (s k) -> p s k", k=256)[:, :, 0:64],
                AF.Exp, bias=0.0, scale=0.125), waits=[("t", sc_done)])
            v_op(lambda h: h.tensor_copy(
                probs_g[:].rearrange("p (s k) -> p s k", k=128)[:, :, 0:64],
                pgc[:].rearrange("p (s k) -> p s k", k=64)),
                 waits=[("s", P.n("s"))], big=True)
            s_op(lambda h: h.activation(
                probs_l[:, :],
                psA[:].rearrange("p (s k) -> p s k", k=256)[:, :, 64:192],
                AF.Exp, bias=0.0, scale=0.125))
            exp_done = P.n("s")
            P.dmat(probsT_g.rearrange("p (s k) -> p s k", s=16),
                   probs_g[:].rearrange("p (s k) -> p s k", s=16),
                   waits=([("v", P.n("v"))] +
                          ([("t", av_prev)] if w_ > 0 else [])))
            trg = P.q("qX")
            P.dmat(probsT_l.rearrange("p (s k) -> p s k", s=16),
                   probs_l[:].rearrange("p (s k) -> p s k", s=16),
                   waits=[("s", P.n("s"))])
            trl = P.q("qX")
            tr_done = trl
            if DEBUG == 7 and ti == 0 and w_ == 0:
                P.waitonly("g", [("qX", tr_done), ("s", exp_done)])
                P.dma("g", "qS", y_out[0:128, 0:1024], probsT_l[:, 0:1024])
                P.dma("g", "qS", y_out[128:256, 0:1024], probsT_l[:, 1024:2048])
                P.dma("g", "qS", y_out[256:384, 0:1024], probsT_g[:, 0:1024])
                P.dma("g", "qS", y_out[384:512, 0:1024], probsT_g[:, 1024:2048])
                P.dma("g", "qS", y_out[512:640, 0:1024], probs_g[:, 0:1024])
                P.dma("g", "qS", y_out[640:768, 0:1024], probs_g[:, 1024:2048])
                P.waitonly("s", [("qS", P.q("qS"))])
                P.waitonly("t", [("qS", P.q("qS"))])
                P.waitonly("v", [("qS", P.q("qS"))])
            P.waitonly("t", [("qX", trl), ("s", exp_done)])
            for s_ in range(16):
                c, p_ = s_ // 2, s_ % 2
                vv = (vTe if p_ == 0 else vTo)[c]
                P.mm(lambda h, s_=s_, vv=vv, w_=w_: h.matmul(
                    psA[:, 256 * s_:256 * s_ + 65],
                    probsT_l[:, s_ * 128:(s_ + 1) * 128],
                    vv[:, w_ * 80:w_ * 80 + 65], start=True, stop=False))
                P.mm(lambda h, s_=s_, c=c, p_=p_: h.matmul(
                    psA[:, 256 * s_:256 * s_ + 65],
                    probsT_g[0:64, s_ * 128:(s_ + 1) * 128],
                    vgT[:, c * 160 + p_ * 80:c * 160 + p_ * 80 + 65],
                    start=False, stop=True), inc=(s_ == 15))
            av_done = P.n("t")
            v_op(lambda h: h.tensor_copy(
                attn_un[:].rearrange("p (s k) -> p s k", k=65),
                psA[:].rearrange("p (s k) -> p s k", k=256)[:, :, 0:65]),
                waits=[("t", av_done)], big=True)
            evac_prev = P.n("v")
            av_prev = av_done
            v_op(lambda h: h.reciprocal(
                srec[:].rearrange("p (s k) -> p s k", k=1),
                attn_un[:].rearrange("p (s k) -> p s k", k=65)[:, :, 64:65]))
            v_op(lambda h, o=attn_tm[w_]: h.tensor_tensor(
                o[:].rearrange("p (s k) -> p s k", k=64),
                attn_un[:].rearrange("p (s k) -> p s k", k=65)[:, :, 0:64],
                srec[:, :, None].broadcast_to([128, 16, 64]), OP.mult))
            if DEBUG == 4 and ti == 0 and w_ == 0:
                P.dma("g", "qS", y_out[0:128, :], probs_l[:, 0:1024],
                      waits=[("v", P.n("v"))])
                P.dma("g", "qS", y_out[128:256, :], probs_l[:, 1024:2048])
                P.dma("g", "qS", y_out[256:384, :], probs_g[:, 0:1024])
                P.dma("g", "qS", y_out[384:512, :], probs_g[:, 1024:2048])
                P.dma("g", "qS", y_out[512:640, :], attn_un[:, 0:1024])
                P.dma("g", "qS", y_out[640:768, 0:16], srec[:, :])
                P.waitonly("s", [("qS", P.q("qS"))])
                P.waitonly("t", [("qS", P.q("qS"))])
                P.waitonly("v", [("qS", P.q("qS"))])

        if STAGE < 5:
            break
        if DEBUG == 3 and ti == 0:
            for g_ in range(NTG):
                P.dma("g", "qS", y_out[g_ * 128:(g_ + 1) * 128, :],
                      attn_tm[g_][:, :], waits=[("v", P.n("v"))])

        P.enter_phase()
        for g_ in range(NTG):
            P.dmat(parena[:, 0:DFF].rearrange("p (i t) -> p i t", i=8)[
                       :, :, g_ * 128:(g_ + 1) * 128],
                   attn_tm[g_][:].rearrange("p (i k) -> p i k", i=8),
                   waits=[("v", P.n("v"))])
        afm_ready = P.q("qX")
        P.waitonly("t", [("qS", wo_ready), ("qX", afm_ready)])
        evcv = {}
        for oc in range(8):
            slot = (oc % 2) * 512
            if oc >= 2:
                P.waitonly("t", [("s", evcv[oc - 2])])
            for ic in range(8):
                P.mm(lambda h, oc=oc, ic=ic, slot=slot: h.matmul(
                    psA[:, slot:slot + 512],
                    wbuf[ic][:, oc * 128:(oc + 1) * 128],
                    afmT[ic][:, :], start=(ic == 0), stop=(ic == 7)),
                    inc=(ic == 7))
            s_op(lambda h, oc=oc, slot=slot: h.copy(
                devac[oc][:, :], psA[:, slot:slot + 512]),
                 waits=[("t", P.n("t"))])
            evcv[oc] = P.n("s")
        for g_ in range(NTG):
            for oc in range(8):
                P.mm(lambda h, g_=g_, oc=oc: h.transpose(
                    psA[:, 2048 + oc * 128:2176 + oc * 128],
                    devac[oc][:, g_ * 128:(g_ + 1) * 128], idf_sb[:]),
                    waits=([("s", P.n("s")), ("v", P.n("v"))]
                           if (g_ == 0 and oc == 0) else
                           ([("v", P.n("v"))] if oc == 0 else None)),
                    inc=(oc == 7))
            v_op(lambda h, g_=g_: h.tensor_tensor(
                xt[g_][:, :], xt[g_][:, :], psA[:, 2048:3072], OP.add),
                 waits=[("t", P.n("t"))], big=True)
            ln_accum(xt[g_][:, :], g_)
            v_op(lambda h, x=xt[g_], o=stat[:, 12 + g_:13 + g_]:
                 h.tensor_reduce(o, x[:, :], AX.X, OP.max))
            v_op(lambda h, x=xt[g_], o=stat[:, 16 + g_:17 + g_]:
                 h.tensor_reduce(o, x[:, :], AX.X, OP.min))

        if DEBUG == 1:
            for g_ in range(NTG):
                P.dma("g", "qS", y_out[t0 + g_ * 128:t0 + (g_ + 1) * 128, :],
                      xt[g_][:, :], waits=[("v", P.n("v"))])

        P.enter_phase()
        for je in range(2):
            gq = ("gA", "gB")[je]
            P.dma("g", gq,
                  ternG_v[je].rearrange("p (i k) -> p i k", i=8),
                  S_tern[0].rearrange("(i p) (e k) -> p i e k",
                                      p=128, e=8)[:, :, je],
                  waits=[("t", P.n("t"))] if je == 0 else None)
            P.dma("g", gq,
                  ternU_v[je].rearrange("p (i k) -> p i k", i=8),
                  S_tern[1].rearrange("(i p) (e k) -> p i e k",
                                      p=128, e=8)[:, :, je])
        C = lambda a, b: stat[:, a:b]
        v_op(lambda h: h.tensor_copy(C(44, 48), C(16, 20)))
        v_op(lambda h: h.tensor_copy(C(40, 44), C(12, 16)))
        ln_chain4()
        v_op(lambda h: h.tensor_tensor(C(12, 16), C(40, 44), C(8, 12),
                                       OP.subtract))
        v_op(lambda h: h.tensor_tensor(C(16, 20), C(8, 12), C(44, 48),
                                       OP.subtract))
        v_op(lambda h: h.tensor_tensor(C(12, 16), C(12, 16), C(16, 20),
                                       OP.max))
        v_op(lambda h: h.tensor_tensor(C(12, 16), C(12, 16), C(20, 24),
                                       OP.mult))
        v_op(lambda h: h.tensor_scalar(C(12, 16), C(12, 16), 1e-5, None,
                                       OP.max))
        v_op(lambda h: h.reciprocal(C(16, 20), C(12, 16)))
        v_op(lambda h: h.tensor_scalar(C(16, 20), C(16, 20), 127.0, None,
                                       OP.mult))
        v_op(lambda h: h.tensor_scalar(C(12, 16), C(12, 16), 1.0 / 127.0,
                                       None, OP.mult))
        v_op(lambda h: h.tensor_tensor(C(36, 40), C(20, 24), C(16, 20),
                                       OP.mult))
        v_op(lambda h: h.scalar_tensor_tensor(C(40, 44), C(8, 12), -1.0,
                                              C(36, 40), OP.mult, OP.mult))
        v_op(lambda h: h.tensor_scalar(
            fstat[:].rearrange("p (g k) -> p g k", k=4)[:, :, 0:1],
            C(12, 16).rearrange("p (g k) -> p g k", k=1), fw_sb[:, 0:1],
            None, OP.mult))
        v_op(lambda h: h.tensor_scalar(
            fstat[:].rearrange("p (g k) -> p g k", k=4)[:, :, 1:2],
            C(12, 16).rearrange("p (g k) -> p g k", k=1), fw_sb[:, 1:2],
            None, OP.mult))
        xq2_cnt = {}
        for g_ in range(NTG):
            v_op(lambda h, x=xt[g_], g_=g_: h.tensor_scalar(
                scr2[:, :], x[:, :], stat[:, 36 + g_:37 + g_],
                stat[:, 40 + g_:41 + g_], OP.mult, OP.add))
            v_op(lambda h: h.tensor_scalar(scr2[:, :], scr2[:, :], C32, C32,
                                           OP.add, OP.subtract))
            v_op(lambda h: h.tensor_scalar(scr[:, :], scr2[:, :], C64, C64,
                                           OP.add, OP.subtract))
            v_op(lambda h: h.scalar_tensor_tensor(
                tmp_tm[:, :], scr2[:, :], omw_sb[:, 0:1], scr[:, :],
                OP.mult, OP.add),
                 waits=[("qX", P.q("qX"))] if g_ > 0 else None)
            P.dmat(lnTA[:].rearrange("p (i t) -> p i t", i=8)[
                       :, :, g_ * 128:(g_ + 1) * 128],
                   tmp_tm[:].rearrange("p (i k) -> p i k", i=8),
                   waits=[("v", P.n("v"))])
            xq2_cnt[g_] = P.q("qX")
        xq2_ready = P.q("qX")

        if STAGE < 6:
            break
        P.enter_phase()
        je_cons = {}
        slot_done = {}
        for je in range(8):
            par = je % 2
            gq = ("gA", "gB")[par]
            if je >= 2:
                P.waitonly("g", [("t", je_cons[je - 2])])
                P.dma("g", gq,
                      ternG_v[par].rearrange("p (i k) -> p i k", i=8),
                      S_tern[0].rearrange("(i p) (e k) -> p i e k",
                                          p=128, e=8)[:, :, je])
                P.dma("g", gq,
                      ternU_v[par].rearrange("p (i k) -> p i k", i=8),
                      S_tern[1].rearrange("(i p) (e k) -> p i e k",
                                          p=128, e=8)[:, :, je])
            P.waitonly("t", [(gq, P.q(gq)),
                             ("qX", xq2_cnt[0] if je == 0 else xq2_ready)])
            for g_ in range(NTG):
                gslot = (g_ % 2) * 1024
                uslot = gslot + 512
                if je == 0 and g_ > 0:
                    P.waitonly("t", [("qX", xq2_cnt[g_])])
                if (g_ % 2) in slot_done:
                    P.waitonly("t", [("v", slot_done[g_ % 2][0]),
                                     ("s", slot_done[g_ % 2][1])])
                for ic in range(8):
                    P.mm(lambda h, g_=g_, ic=ic, par=par, gslot=gslot: h.matmul(
                        psA[:, gslot:gslot + 512],
                        lnT[ic][:, g_ * 128:(g_ + 1) * 128],
                        ternG_v[par][:, ic * 512:(ic + 1) * 512],
                        start=(ic == 0), stop=(ic == 7)))
                    P.mm(lambda h, g_=g_, ic=ic, par=par, uslot=uslot: h.matmul(
                        psA[:, uslot:uslot + 512],
                        lnT[ic][:, g_ * 128:(g_ + 1) * 128],
                        ternU_v[par][:, ic * 512:(ic + 1) * 512],
                        start=(ic == 0), stop=(ic == 7)), inc=(ic == 7))
                s_op(lambda h, g_=g_, gslot=gslot: h.activation(
                    scr[:, 0:512], psA[:, gslot:gslot + 512], AF.Silu,
                    bias=0.0, scale=fstat[:, g_ * 4:g_ * 4 + 1]),
                     waits=[("t", P.n("t")), ("v", P.n("v"))])
                v_op(lambda h, g_=g_, je=je, uslot=uslot: h.tensor_tensor(
                    hbuf[g_][:, je * 512:(je + 1) * 512], scr[:, 0:512],
                    psA[:, uslot:uslot + 512], OP.mult),
                     waits=[("s", P.n("s"))], big=True)
                slot_done[g_ % 2] = (P.n("v"), P.n("s"))
                if je == 0:
                    v_op(lambda h, g_=g_: h.tensor_reduce(
                        stat[:, 24 + g_:25 + g_], hbuf[g_][:, 0:512],
                        AX.X, OP.max, apply_absolute_value=True))
                else:
                    v_op(lambda h, g_=g_, je=je: h.tensor_reduce(
                        stat[:, 28:29], hbuf[g_][:, je * 512:(je + 1) * 512],
                        AX.X, OP.max, apply_absolute_value=True))
                    v_op(lambda h, g_=g_: h.tensor_tensor(
                        stat[:, 24 + g_:25 + g_], stat[:, 24 + g_:25 + g_],
                        stat[:, 28:29], OP.max))
            je_cons[je] = P.n("t")

        P.enter_phase()
        for j2 in range(2):
            P.dma("g", ("wA", "wB")[j2], wdb[:, j2 * D:(j2 + 1) * D],
                  S_tern[2][j2 * 128:(j2 + 1) * 128, :])
        kapr = fstat[:].rearrange("p (g k) -> p g k", k=4)
        for h_ in range(2):
            c0 = 12 + 2 * h_
            s0 = 16 + 2 * h_
            v_op(lambda h, c0=c0, h_=h_: h.tensor_tensor(
                stat[:, c0:c0 + 2].rearrange("p (g k) -> p g k", k=1),
                stat[:, 24 + 2 * h_:26 + 2 * h_].rearrange(
                    "p (g k) -> p g k", k=1),
                kapr[:, 2 * h_:2 * h_ + 2, 1:2], OP.mult))
            v_op(lambda h, c0=c0: h.tensor_scalar(
                stat[:, c0:c0 + 2], stat[:, c0:c0 + 2], 1e-5, None, OP.max))
            v_op(lambda h, s0=s0, c0=c0: h.reciprocal(
                stat[:, s0:s0 + 2], stat[:, c0:c0 + 2]))
            v_op(lambda h, s0=s0, h_=h_: h.tensor_tensor(
                stat[:, s0:s0 + 2].rearrange("p (g k) -> p g k", k=1),
                stat[:, s0:s0 + 2].rearrange("p (g k) -> p g k", k=1),
                kapr[:, 2 * h_:2 * h_ + 2, 1:2], OP.mult))
            v_op(lambda h, s0=s0: h.tensor_scalar(
                stat[:, s0:s0 + 2], stat[:, s0:s0 + 2], 127.0, None,
                OP.mult))
            v_op(lambda h, h_=h_, c0=c0: h.tensor_scalar(
                kapr[:, 2 * h_:2 * h_ + 2, 2:3],
                stat[:, c0:c0 + 2].rearrange("p (g k) -> p g k", k=1),
                fw_sb[:, 2:3], 1.0 / 127.0, OP.mult, OP.mult))
            for g_ in (2 * h_, 2 * h_ + 1):
                v_op(lambda h, b=hbuf[g_], g_=g_: h.tensor_scalar(
                    b[:, :], b[:, :], stat[:, 16 + g_:17 + g_], C32,
                    OP.mult, OP.add))
                v_op(lambda h, b=hbuf[g_]: h.tensor_scalar(
                    b[:, :], b[:, :], C32, None, OP.subtract))
                v_op(lambda h, b=hbuf[g_]: h.tensor_scalar(
                    rsb[:, :], b[:, :], C64, C64, OP.add, OP.subtract))
                v_op(lambda h, b=hbuf[g_]: h.scalar_tensor_tensor(
                    b[:, :], b[:, :], omw_sb[:, 0:1], rsb[:, :],
                    OP.mult, OP.add))
                P.dma("g", "qS", S_hq[g_ * 128:(g_ + 1) * 128, :],
                      hbuf[g_][:, :], waits=[("v", P.n("v"))])
        hq_done = P.q("qS")

        P.enter_phase()
        mmcnt = {}
        qr8 = ("hA", "hB", "sA", "sB", "pA", "pB", "gA", "gB")
        for jc in range(32):
            par = jc % 2
            sl = (jc % 8) * TT
            hq_q = qr8[jc % 8]
            wd_q = ("wA", "wB")[par]
            if jc >= 8:
                P.waitonly("y", [("t", mmcnt[jc - 8])])
            if jc >= 2:
                P.waitonly("g", [("t", mmcnt[jc - 2])])
            P.dmat(lnTA[:, sl:sl + TT],
                   S_hq[0:TT, jc * 128:(jc + 1) * 128],
                   waits=[("qS", hq_done)] if jc < 8 else None, qn=hq_q)
            if jc >= 2:
                P.dma("g", wd_q, wdb[:, par * D:(par + 1) * D],
                      S_tern[2][jc * 128:(jc + 1) * 128, :])
            P.waitonly("t", [(hq_q, P.q(hq_q)), (wd_q, P.q(wd_q))])
            for g_ in range(NTG):
                for nh in range(2):
                    P.mm(lambda h, jc=jc, g_=g_, nh=nh, sl=sl, par=par:
                         h.matmul(
                        psA[:, g_ * 1024 + nh * 512:g_ * 1024 + nh * 512 + 512],
                        lnTA[:, sl + g_ * 128:sl + (g_ + 1) * 128],
                        wdb[:, par * D + nh * 512:par * D + (nh + 1) * 512],
                        start=(jc == 0), stop=(jc == 31)),
                        inc=(g_ == 3 and nh == 1))
            mmcnt[jc] = P.n("t")
        dn_done = P.n("t")
        if ti + 1 < NTILE and STAGE >= 8 and not DEBUG:
            t1 = t0 + TT
            nst = {}
            for g_ in range(NTG):
                pb = (scr, scr2)[g_ % 2]
                pq = ("qL", "qX")[g_ % 2]
                if g_ >= 2:
                    P.waitonly("y", [("v", nst[g_ - 2])])
                P.dma("y", pq, pb[:, :],
                      x_in[t1 + g_ * 128:t1 + (g_ + 1) * 128, :])
                v_op(lambda h, x=pb[:, :], g_=g_: h.tensor_scalar(
                    hbuf[0][:, 0:D], x, 1.0, 0.0, OP.mult, OP.add,
                    accum_out=stat[:, g_:g_ + 1]),
                     waits=([(pq, P.q(pq))] +
                            ([("qS", hq_done)] if g_ == 0 else [])),
                     big=True)
                v_op(lambda h, x=pb[:, :], g_=g_: h.scalar_tensor_tensor(
                    hbuf[0][:, D:2 * D], x, 1.0, x, OP.mult, OP.mult,
                    accum_out=stat[:, 4 + g_:5 + g_]), big=True)
                nst[g_] = P.n("v")
        for g_ in range(NTG):
            v_op(lambda h, g_=g_: h.scalar_tensor_tensor(
                xt[g_][:, :], psA[:, g_ * 1024:(g_ + 1) * 1024],
                fstat[:, g_ * 4 + 2:g_ * 4 + 3],
                xt[g_][:, :], OP.mult, OP.add),
                 waits=[("t", dn_done)] if g_ == 0 else None, big=True)
            if not DEBUG:
                P.dma("g", "qS", y_out[t0 + g_ * 128:t0 + (g_ + 1) * 128, :],
                      xt[g_][:, :], waits=[("v", P.n("v"))])
            elif DEBUG == 2:
                pass

    if STAGE < 7:
        P.enter_phase()
        for g_ in range(NTG):
            P.dma("g", "qS", y_out[g_ * 128:(g_ + 1) * 128, :], scr[:, :],
                  waits=[("v", P.n("v")), ("s", P.n("s")), ("t", P.n("t"))])

    P.enter_phase()
    P.waitonly("g", [(qn, P.q(qn)) for qn in P.QUEUES if P.q(qn) > 0])

    P.finalize()
    with es, nc.Block() as block:
        block.vector(lambda h: P.replay("v", h))
        block.scalar(lambda h: P.replay("s", h))
        block.tensor(lambda h: P.replay("t", h))
        block.gpsimd(lambda h: P.replay("g", h))
        block.sync(lambda h: P.replay("y", h))
    return nc


_NC_CACHE = {}


def _get_nc():
    if "nc" not in _NC_CACHE:
        nc = bass.Bass("TRN2", target_bir_lowering=False,
                       dynamic_dma_scratch_size=16384)
        _NC_CACHE["nc"] = build(nc)
    return _NC_CACHE["nc"]


def kernel(L0_attn_x, L0_attn_Wq, L0_attn_Wk, L0_attn_Wv, L0_attn_Wo,
           L0_ffn_w_gate, L0_ffn_w_up, L0_ffn_w_down):
    x = np.ascontiguousarray(np.asarray(L0_attn_x, dtype=np.float32))
    B, T, _ = x.shape
    WT = {
        "WqT": np.ascontiguousarray(np.asarray(L0_attn_Wq, np.float32).T),
        "WkT": np.ascontiguousarray(np.asarray(L0_attn_Wk, np.float32).T),
        "WvT": np.ascontiguousarray(np.asarray(L0_attn_Wv, np.float32).T),
        "WoT": np.ascontiguousarray(np.asarray(L0_attn_Wo, np.float32).T),
        "WgT": np.ascontiguousarray(np.asarray(L0_ffn_w_gate, np.float32).T),
        "WuT": np.ascontiguousarray(np.asarray(L0_ffn_w_up, np.float32).T),
        "WdT": np.ascontiguousarray(np.asarray(L0_ffn_w_down, np.float32).T),
    }
    cst = np.zeros((128, 8), np.float32)
    cst[:, 0] = np.float32(2.0**-30)
    cst[:, 1] = 1.0
    idf = np.eye(128, dtype=np.float32)

    nc = _get_nc()
    in_maps = []
    for c in range(8):
        b, hf = c // 2, c % 2
        m = {"x": np.ascontiguousarray(x[b, hf * TCORE:(hf + 1) * TCORE, :]),
             "xg": np.ascontiguousarray(x[b, 0:G, :]),
             "cst": cst, "idf": idf}
        m.update(WT)
        in_maps.append(m)
    res = run_bass_kernel_spmd(nc, in_maps, core_ids=list(range(8)))
    out = np.empty((B, T, D), np.float32)
    for c in range(8):
        b, hf = c // 2, c % 2
        out[b, hf * TCORE:(hf + 1) * TCORE, :] = res.results[c]["y"]
    return out


# version 30 (final)
# speedup vs baseline: 1.0327x; 1.0327x over previous
import numpy as np
import concourse.bass as bass
import concourse.mybir as mybir
from concourse.bass_utils import run_bass_kernel_spmd

F32 = mybir.dt.float32
BF16 = mybir.dt.bfloat16
FP8 = mybir.dt.float8e4
AF = mybir.ActivationFunctionType
OP = mybir.AluOpType
AX = mybir.AxisListType

C32 = float(1.5 * 2.0**23)
C64 = float(1.5 * 2.0**52)
EPS_LN = 1e-5
D = 1024
DFF = 4096
G = 64
TCORE = 4096
TT = 512
DEBUG = False
STAGE = 99
SUB = 99
NTILE = TCORE // TT
NTG = TT // 128


class Prog:

    QUEUES = ("qL", "qX", "qS", "pA", "pB", "sA", "sB",
              "gA", "gB", "hA", "hB", "wA", "wB")
    CENG = ("v", "s", "t", "g")

    def __init__(self, nc, es):
        self.nc = nc
        self.sem = {}
        for name in ("cv", "cs", "ct", "cg", "qL", "qX", "qS", "pA", "pB",
                     "sA", "sB", "gA", "gB", "hA", "hB", "wA", "wB"):
            self.sem[name] = es.enter_context(nc.semaphore(name))
        self.esem = {"v": "cv", "s": "cs", "t": "ct", "g": "cg"}
        self.ops = {e: [] for e in ("v", "s", "t", "g", "y")}
        self.nreal = {e: 0 for e in ("v", "s", "t", "g", "y")}
        self.qcnt = {q: 0 for q in self.QUEUES}
        self.entered = {}
        self.phase_id = 0
        self.snap = {}

    def n(self, eng):
        return self.nreal[eng]

    def q(self, qname):
        return self.qcnt[qname]

    def enter_phase(self):
        self.phase_id += 1
        self.snap = dict(self.qcnt)
        for e in self.CENG:
            self.snap[e] = self.nreal[e]

    def _waits(self, eng, extra):
        w = []
        if self.entered.get(eng) != self.phase_id:
            self.entered[eng] = self.phase_id
            for k, v in self.snap.items():
                if k != eng and v > 0:
                    w.append((k, v))
        if extra:
            w += extra
        return w

    def op(self, eng, emit, waits=None, small=True):
        w = self._waits(eng, waits)
        if small and eng in self.CENG and self.nreal[eng] > 0:
            w = w + [(eng, self.nreal[eng])]
        ridx = self.nreal.get(eng)
        if eng in self.CENG:
            self.nreal[eng] += 1
        self.ops[eng].append([w, emit, None, ridx])

    def mm(self, emit, waits=None, inc=False):
        self.op("t", emit, waits, small=False)

    def dma(self, eng, qn, out, in_, waits=None):
        w = self._waits(eng, waits)
        self.ops[eng].append(
            [w, lambda h, o=out, i=in_: h.dma_start(o, i), qn, None])
        self.qcnt[qn] += 1

    def dmat(self, out, in_, waits=None, qn="qX", eng="y"):
        w = self._waits(eng, waits)
        self.ops[eng].append(
            [w, lambda h, o=out, i=in_: h.dma_start_transpose(o, i), qn,
             None])
        self.qcnt[qn] += 1

    def waitonly(self, eng, waits):
        self.ops[eng].append([self._waits(eng, waits), None, None, None])

    def finalize(self):
        targets = {e: set() for e in self.CENG}
        for e in self.ops:
            for w, _, _, _ in self.ops[e]:
                for (k, v) in w:
                    if k in self.CENG and v > 0:
                        targets[k].add(v - 1)
        self.incpos = {}
        self.rank = {}
        for e in self.CENG:
            pos = sorted(targets[e])
            self.incpos[e] = set(pos)
            r = {}
            cnt = 0
            for j in pos:
                cnt += 1
                r[j] = cnt
            self.rank[e] = r

    def _wait_val(self, k, v):
        if k in self.QUEUES:
            return self.sem[k], 16 * v
        return self.sem[self.esem[k]], self.rank[k][v - 1]

    def replay(self, eng, h):
        is_ceng = eng in self.CENG
        for (waits, emit, tag, ridx) in self.ops[eng]:
            for (k, v) in waits:
                if v == 0:
                    continue
                sem, val = self._wait_val(k, v)
                h.wait_ge(sem, val)
            if emit is None:
                continue
            ins = emit(h)
            if tag in self.QUEUES:
                ins.then_inc(self.sem[tag], 16)
            elif is_ceng and ridx in self.incpos[eng]:
                ins.then_inc(self.sem[self.esem[eng]], 1)


def build(nc):
    from contextlib import ExitStack
    es = ExitStack()
    sbuf = lambda n_, s_, d_=F32: es.enter_context(nc.sbuf_tensor(n_, s_, d_))

    x_in = nc.dram_tensor("x", [TCORE, D], F32, kind="ExternalInput").ap()
    xg_in = nc.dram_tensor("xg", [G, D], F32, kind="ExternalInput").ap()
    Wins = {}
    for n, r, c in (("WqT", D, D), ("WkT", D, D), ("WvT", D, D), ("WoT", D, D),
                    ("WgT", D, DFF), ("WuT", D, DFF), ("WdT", DFF, D)):
        Wins[n] = nc.dram_tensor(n, [r, c], F32, kind="ExternalInput").ap()
    cst_in = nc.dram_tensor("cst", [128, 8], F32, kind="ExternalInput").ap()
    idf_in = nc.dram_tensor("idf", [128, 128], F32, kind="ExternalInput").ap()
    y_out = nc.dram_tensor("y", [TCORE, D], F32, kind="ExternalOutput").ap()
    S_tern = [nc.dram_tensor("Stg", [D, DFF], FP8).ap(),
              nc.dram_tensor("Stu", [D, DFF], FP8).ap(),
              nc.dram_tensor("Std", [DFF, D], FP8).ap()]
    S_hq = nc.dram_tensor("Shq", [TT, DFF], BF16).ap()

    cst_sb = sbuf("cst_sb", [128, 8])
    idf_sb = sbuf("idf_sb", [128, 128])
    w_sb = sbuf("w_sb", [128, 1])
    omw_sb = sbuf("omw_sb", [128, 1])
    onesf = sbuf("onesf", [1, 128])
    onesc = sbuf("onesc", [128, 1])
    fw_sb = sbuf("fw_sb", [128, 4])
    sw_sb = sbuf("sw_sb", [128, 4])
    kg_ev = sbuf("kg_ev", [128, 8 * G], BF16)
    kg_od = sbuf("kg_od", [128, 8 * G], BF16)
    vgT = sbuf("vgT", [G, 8 * 160], BF16)
    stat = sbuf("stat", [128, 48])
    fstat = sbuf("fstat", [128, 16])
    scr = sbuf("scr", [128, D])
    scr2 = sbuf("scr2", [128, D])
    xgl = sbuf("xgl", [G, D], BF16)
    xg_ft = sbuf("xg_ft", [G, D])
    xg_f = xg_ft[0:G, :]
    xgT = sbuf("xgT", [128, 8 * G], BF16)
    vg_f = sbuf("vg_f", [128, G])

    xtA = sbuf("xtA", [128, NTG * D])
    xt = [xtA[:, i * D:(i + 1) * D] for i in range(NTG)]
    lnTA = sbuf("lnTA", [128, 8 * TT], BF16)
    lnT = [lnTA[:, i * TT:(i + 1) * TT] for i in range(8)]
    tmp_tm = sbuf("tmp_tm", [128, D], BF16)
    karena = sbuf("karena", [128, 2 * DFF], BF16)
    qarena = sbuf("qarena", [128, DFF], BF16)
    varena = sbuf("varena", [128, DFF], BF16)
    warena = sbuf("warena", [128, 2 * DFF], BF16)
    parena = sbuf("parena", [128, DFF], BF16)
    atarena = sbuf("atarena", [128, DFF], BF16)
    kev = [karena[:, c * TT:(c + 1) * TT] for c in range(8)]
    kod = [karena[:, DFF + c * TT:DFF + (c + 1) * TT] for c in range(8)]
    q_fm = [qarena[:, c * TT:(c + 1) * TT] for c in range(8)]
    v_fm = [varena[:, c * TT:(c + 1) * TT] for c in range(8)]
    ternU_v = [qarena[:, :], varena[:, :]]
    wbuf = [warena[:, r * D:(r + 1) * D] for r in range(8)]
    ternG_v = [warena[:, 0:DFF], warena[:, DFF:2 * DFF]]
    probsT_l = parena[:, 0:16 * 128]
    probsT_g = parena[:, 16 * 128:32 * 128]
    afmT = [parena[:, ic * TT:(ic + 1) * TT] for ic in range(8)]
    attn_tm = [atarena[:, g * D:(g + 1) * D] for g in range(NTG)]
    hqtm = atarena[:, :]
    tstg = atarena[:, 0:2 * D]
    rsb = qarena[:, :]
    vTe = [sbuf(f"vTe{i}", [128, NTG * 80], BF16) for i in range(8)]
    vTo = [sbuf(f"vTo{i}", [128, NTG * 80], BF16) for i in range(8)]
    probs_l = sbuf("probs_l", [128, 16 * 128], BF16)
    pgc = sbuf("pgc", [128, 16 * 64], BF16)
    probs_g = sbuf("probs_g", [128, 16 * 128], BF16)
    attn_un = sbuf("attn_un", [128, 16 * 65])
    srec = sbuf("srec", [128, 16])
    pevac = sbuf("pevac", [128, TT])
    hbuf = [sbuf(f"hbuf{i}", [128, DFF]) for i in range(NTG)]
    devac = [hbuf[oc % 4][:, (oc // 4) * 2048:(oc // 4) * 2048 + TT]
             for oc in range(8)]
    hqT_b = sbuf("hqT_b", [128, 2 * TT], BF16)
    wdb = sbuf("wdb", [128, 2 * D], BF16)

    psA = es.enter_context(nc.psum_tensor("psA", [128, 4096], F32))

    P = Prog(nc, es)

    def v_op(emit=None, waits=None, big=False):
        P.op("v", emit, waits, small=True)

    def s_op(emit, waits=None):
        P.op("s", emit, waits, small=False)

    def ln_stats(xtile, pc):
        s0 = stat[0:pc, 0:1]; s1 = stat[0:pc, 1:2]; s2 = stat[0:pc, 2:3]
        s3 = stat[0:pc, 3:4]; s4 = stat[0:pc, 4:5]; s5 = stat[0:pc, 5:6]
        s6 = stat[0:pc, 6:7]; s7 = stat[0:pc, 7:8]; s8 = stat[0:pc, 8:9]
        s9 = stat[0:pc, 9:10]
        dump = scr[0:pc, :]
        v_op(lambda h, o=dump, x=xtile, a=s0: h.tensor_scalar(
            o, x, 1.0, 0.0, OP.mult, OP.add, accum_out=a), big=True)
        v_op(lambda h, o=dump, x=xtile, a=s1: h.scalar_tensor_tensor(
            o, x, 1.0, x, OP.mult, OP.mult, accum_out=a), big=True)
        v_op(lambda h, o=s2, i=s0: h.tensor_scalar(o, i, 1.0 / D, None, OP.mult))
        v_op(lambda h, o=s3, i=s2: h.tensor_scalar(o, i, s2, None, OP.mult))
        v_op(lambda h, o=s4, i=s1, m=s3: h.scalar_tensor_tensor(
            o, i, 1.0 / D, m, OP.mult, OP.subtract))
        v_op(lambda h, o=s4: h.tensor_scalar(o, o, EPS_LN, None, OP.add))
        s_op(lambda h, o=s5, i=s4: h.activation(o, i, AF.Sqrt),
             waits=[("v", P.n("v"))])
        v_op(lambda h, o=s6, i=s5: h.reciprocal(o, i),
             waits=[("s", P.n("s"))])
        v_op(lambda h, o=s7, i=s6: h.tensor_scalar(o, i, s6, None, OP.mult))
        v_op(lambda h, o=s8, i=s7, u=s4: h.scalar_tensor_tensor(
            o, i, -0.5, u, OP.mult, OP.mult))
        v_op(lambda h, o=s8: h.tensor_scalar(o, o, 1.5, None, OP.add))
        v_op(lambda h, o=s9, i=s6, c=s8: h.tensor_scalar(o, i, c, None, OP.mult))

    def g_op(emit=None, waits=None):
        P.op("g", emit, waits, small=True)

    def ln_accum(xtile, col):
        v_op(lambda h, o=scr[:, :], x=xtile, a=stat[:, col:col + 1]:
             h.tensor_scalar(o, x, 1.0, 0.0, OP.mult, OP.add, accum_out=a),
             big=True)
        v_op(lambda h, o=scr[:, :], x=xtile, a=stat[:, 4 + col:5 + col]:
             h.scalar_tensor_tensor(o, x, 1.0, x, OP.mult, OP.mult,
                                    accum_out=a), big=True)

    def ln_chain4():
        C = lambda a, b: stat[:, a:b]
        v_op(lambda h: h.tensor_scalar(C(8, 12), C(0, 4), 1.0 / D, None,
                                       OP.mult))
        v_op(lambda h: h.tensor_tensor(C(12, 16), C(8, 12), C(8, 12), OP.mult))
        v_op(lambda h: h.scalar_tensor_tensor(C(16, 20), C(4, 8), 1.0 / D,
                                              C(12, 16), OP.mult, OP.subtract))
        v_op(lambda h: h.tensor_scalar(C(16, 20), C(16, 20), EPS_LN, None,
                                       OP.add))
        s_op(lambda h: h.activation(C(24, 28), C(16, 20), AF.Sqrt),
             waits=[("v", P.n("v"))])
        v_op(lambda h: h.reciprocal(C(28, 32), C(24, 28)),
             waits=[("s", P.n("s"))])
        v_op(lambda h: h.tensor_tensor(C(32, 36), C(28, 32), C(28, 32),
                                       OP.mult))
        v_op(lambda h: h.scalar_tensor_tensor(C(32, 36), C(32, 36), -0.5,
                                              C(16, 20), OP.mult, OP.mult))
        v_op(lambda h: h.tensor_scalar(C(32, 36), C(32, 36), 1.5, None,
                                       OP.add))
        v_op(lambda h: h.tensor_tensor(C(20, 24), C(28, 32), C(32, 36),
                                       OP.mult))

    P.enter_phase()
    P.dma("y", "qL", cst_sb[:], cst_in)
    P.dma("y", "qL", idf_sb[:], idf_in)
    P.dma("y", "qL", xg_f[:, :], xg_in)
    ld0 = P.q("qL")

    P.waitonly("v", [("qL", ld0)])
    if SUB < 1:
        v_op(lambda h: h.memset(w_sb[:], 0.0))
    if SUB >= 1:
        v_op(lambda h: h.tensor_add(w_sb[:], cst_sb[:, 1:2], cst_sb[:, 0:1]))
    v_op(lambda h: h.tensor_sub(w_sb[:], w_sb[:], cst_sb[:, 1:2]))
    v_op(lambda h: h.tensor_scalar(w_sb[:], w_sb[:], float(2.0**30), None,
                                   OP.mult))
    v_op(lambda h: h.scalar_tensor_tensor(omw_sb[:], w_sb[:], -1.0,
                                          cst_sb[:, 1:2], OP.mult, OP.add))
    v_op(big=True, emit=lambda h: h.memset(onesf[:], 1.0))
    v_op(big=True, emit=lambda h: h.memset(onesc[:], 1.0))
    for c in (range(8) if SUB >= 2 else []):
        v_op(big=True, emit=lambda h, t=kev[c]: h.memset(t[64:128, :], 0.0))
        v_op(big=True, emit=lambda h, t=kod[c]: h.memset(t[0:64, :], 0.0))
    v_op(big=True, emit=lambda h: h.memset(kg_ev[64:128, :], 0.0))
    v_op(big=True, emit=lambda h: h.memset(kg_od[0:64, :], 0.0))
    v_op(big=True, emit=lambda h: h.memset(probs_g[:, :], 0.0))
    for c in (range(8) if SUB >= 3 else []):
        v_op(big=True, emit=lambda h, t=vTe[c]: h.memset(
            t[:].rearrange("p (w k) -> p w k", k=80)[:, :, 64:65], 1.0))
        v_op(big=True, emit=lambda h, t=vTo[c]: h.memset(
            t[:].rearrange("p (w k) -> p w k", k=80)[:, :, 64:65], 1.0))
    v_op(big=True, emit=lambda h: h.memset(
        vgT[:].rearrange("p (c k) -> p c k", k=80)[:, :, 64:65], 1.0))

    for wi, (wname, Sdst, RR, CC) in ([] if STAGE < 1 else enumerate(
            (("WgT", S_tern[0], D, DFF), ("WuT", S_tern[1], D, DFF),
             ("WdT", S_tern[2], DFF, D)))):
        Wsrc = Wins[wname]
        tq8 = ("pA", "pB", "hA", "hB", "wA", "wB", "gA", "gB")
        wide = (CC == DFF)
        if wide:
            chs = [(s, 0) for s in range(RR // 128)]
            bufs = [hbuf[g] for g in range(4)]
            nbuf = 4
        else:
            chs = [(r, cb) for r in range(RR // 128) for cb in range(CC // D)]
            bufs = [scr, scr2] + [hbuf[g][:, 1024:2048] for g in range(4)] + \
                   [hbuf[0][:, 2560:3584], hbuf[1][:, 2560:3584]]
            nbuf = 8
        P.enter_phase()
        cons = {}
        for i, (r, cb) in enumerate(chs):
            pq = tq8[i % nbuf]
            if i >= nbuf:
                P.waitonly("y", [("v", cons[i - nbuf])])
            cw = CC if wide else D
            P.dma("y", pq, bufs[i % nbuf][:, :],
                  Wsrc[r * 128:(r + 1) * 128, cb * cw:(cb + 1) * cw])
            P.waitonly("v", [(pq, P.q(pq))])
            v_op(lambda h, b=bufs[i % nbuf]: h.tensor_reduce(
                stat[:, 12:13], b[:, :], AX.X, OP.add,
                apply_absolute_value=True), big=True)
            if i == 0:
                v_op(lambda h: h.tensor_copy(stat[:, 13:14], stat[:, 12:13]))
            else:
                v_op(lambda h: h.tensor_add(stat[:, 13:14], stat[:, 13:14],
                                            stat[:, 12:13]))
            cons[i] = P.n("v")
        P.waitonly("t", [("v", P.n("v"))])
        P.mm(lambda h: h.matmul(psA[0:1, 0:1], onesc[:], stat[:, 13:14],
                                start=True, stop=True), inc=True)
        P.waitonly("v", [("t", P.n("t"))])
        nelem = float(RR * CC)
        v_op(lambda h, s=nelem: h.tensor_scalar(stat[0:1, 14:15], psA[0:1, 0:1],
                                                1.0 / s, None, OP.mult))
        v_op(lambda h: h.tensor_scalar(stat[0:1, 14:15], stat[0:1, 14:15],
                                       1e-5, None, OP.max))
        v_op(lambda h: h.reciprocal(stat[0:1, 15:16], stat[0:1, 14:15]))
        P.waitonly("t", [("v", P.n("v"))])
        P.mm(lambda h: h.matmul(psA[0:128, 512:514], onesf[:],
                                stat[0:1, 14:16], start=True, stop=True),
             inc=True)
        P.waitonly("v", [("t", P.n("t"))])
        v_op(lambda h, wi=wi: h.tensor_copy(fw_sb[:, wi:wi + 1],
                                            psA[:, 512:513]))
        v_op(lambda h, wi=wi: h.tensor_copy(sw_sb[:, wi:wi + 1],
                                            psA[:, 513:514]))
        P.enter_phase()
        cons2 = {}
        stdone = {}
        tslabs = [atarena[:, :], parena[:, :]] if wide else \
                 [tstg[:, 0:D], tstg[:, D:2 * D]]
        for i, (r, cb) in enumerate(chs):
            pq = tq8[i % nbuf]
            sq = ("sA", "sB")[i % 2]
            if i == 0:
                P.waitonly("y", [("v", P.n("v"))])
            if i >= nbuf:
                P.waitonly("y", [("v", cons2[i - nbuf])])
            cw = CC if wide else D
            P.dma("y", pq, bufs[i % nbuf][:, :],
                  Wsrc[r * 128:(r + 1) * 128, cb * cw:(cb + 1) * cw])
            b = bufs[i % nbuf]
            t = tslabs[i % 2]
            P.waitonly("v", [(pq, P.q(pq))])
            if i >= 2:
                P.waitonly("v", [(sq, stdone[i - 2])])
            v_op(lambda h, b=b, t=t, wi=wi: h.tensor_scalar(
                t[:, :], b[:, :], sw_sb[:, wi:wi + 1], 0.5, OP.mult,
                OP.is_ge), big=True)
            v_op(lambda h, b=b, wi=wi: h.tensor_scalar(
                b[:, :], b[:, :], sw_sb[:, wi:wi + 1], -0.5, OP.mult,
                OP.is_le), big=True)
            v_op(lambda h, b=b, t=t: h.tensor_sub(t[:, :], t[:, :], b[:, :]))
            cons2[i] = P.n("v")
            P.dma("g", sq, Sdst[r * 128:(r + 1) * 128, cb * cw:(cb + 1) * cw],
                  t[:, :], waits=[("v", P.n("v"))])
            stdone[i] = P.q(sq)

    P.enter_phase()
    if STAGE < 2:
        for g_ in range(NTG):
            P.dma("g", "qS", y_out[g_ * 128:(g_ + 1) * 128, :], scr[:, :])
        P.enter_phase()
        P.waitonly("g", [("qS", P.q("qS")), ("qL", P.q("qL")),
                         ("qX", P.q("qX"))])
        P.finalize()
        with es, nc.Block() as block:
            block.vector(lambda h: P.replay("v", h))
            block.scalar(lambda h: P.replay("s", h))
            block.tensor(lambda h: P.replay("t", h))
            block.gpsimd(lambda h: P.replay("g", h))
            block.sync(lambda h: P.replay("y", h))
        return nc
    ln_stats(xg_f[:, :], G)
    v_op(lambda h: h.scalar_tensor_tensor(
        stat[0:G, 10:11], stat[0:G, 2:3], -1.0, stat[0:G, 9:10],
        OP.mult, OP.mult))
    v_op(lambda h: h.tensor_scalar(xgl[:, :], xg_f[:, :], stat[0:G, 9:10],
                                   stat[0:G, 10:11], OP.mult, OP.add))
    P.dmat(xgT[:].rearrange("p (i g) -> p i g", i=8),
           xgl[:, :].rearrange("p (i k) -> p i k", i=8),
           waits=[("v", P.n("v"))])
    for widx, isK in ((1, True), (2, False)):
        P.enter_phase()
        wn = ("WqT", "WkT", "WvT", "WoT")[widx]
        for r in range(8):
            P.dma("g", "qS", wbuf[r][:, :], Wins[wn][r * 128:(r + 1) * 128, :])
        wready = P.q("qS")
        P.waitonly("t", [("qS", wready), ("qX", P.q("qX")),
                         ("v", P.n("v"))])
        evcv = {}
        for oc in range(8):
            slot = (oc % 2) * 512
            if oc >= 2:
                P.waitonly("t", [("v", evcv[oc - 2])])
            for ic in range(8):
                P.mm(lambda h, oc=oc, ic=ic, slot=slot: h.matmul(
                    psA[:, slot:slot + G],
                    wbuf[ic][:, oc * 128:(oc + 1) * 128],
                    xgT[:, ic * G:(ic + 1) * G],
                    start=(ic == 0), stop=(ic == 7)), inc=(ic == 7))
            if isK:
                v_op(lambda h, oc=oc, slot=slot: h.tensor_copy(
                    kg_ev[0:64, oc * G:(oc + 1) * G], psA[0:64, slot:slot + G]),
                    waits=[("t", P.n("t"))], big=True)
                v_op(lambda h, oc=oc, slot=slot: h.tensor_copy(
                    kg_od[64:128, oc * G:(oc + 1) * G],
                    psA[64:128, slot:slot + G]), big=True)
            else:
                v_op(lambda h, slot=slot: h.tensor_copy(
                    vg_f[:, :], psA[:, slot:slot + G]),
                    waits=[("t", P.n("t"))])
                P.mm(lambda h: h.transpose(psA[0:64, 1024:1152], vg_f[:, :],
                                           idf_sb[:]),
                     waits=[("v", P.n("v"))], inc=True)
                v_op(lambda h, oc=oc: h.tensor_copy(
                    vgT[:, oc * 160:oc * 160 + 64], psA[0:64, 1024:1088]),
                    waits=[("t", P.n("t"))])
                v_op(lambda h, oc=oc: h.tensor_copy(
                    vgT[:, oc * 160 + 80:oc * 160 + 144], psA[0:64, 1088:1152]))
            evcv[oc] = P.n("v")

    if DEBUG == 5:
        P.enter_phase()
        P.dma("g", "qS", y_out[0:128, 0:512], kg_ev[:, :],
              waits=[("v", P.n("v")), ("t", P.n("t")), ("s", P.n("s"))])
        P.dma("g", "qS", y_out[128:256, 0:512], kg_od[:, :])
        P.dma("g", "qS", y_out[256:384, 0:512], xgT[:, :])
        P.dma("g", "qS", y_out[384:448, 0:1024], xgl[:, :])
        P.dma("g", "qS", y_out[448:512, 0:1024], vgT[:, :].rearrange(
            "p (a b) -> p a b", b=160)[:, 0:6].rearrange("p a b -> p (a b)")
              if False else vgT[:, 0:1024])

    _ntile = 0 if STAGE < 3 else (1 if STAGE < 8 else NTILE)
    for ti in range(_ntile):
        t0 = ti * TT
        P.enter_phase()
        for g_ in range(NTG):
            P.dma("y", "qL", xt[g_][:, :],
                  x_in[t0 + g_ * 128:t0 + (g_ + 1) * 128, :])
        xld = P.q("qL")
        for r in range(8):
            P.dma("g", "qS", wbuf[r][:, :], Wins["WqT"][r * 128:(r + 1) * 128, :])
        wq_ready = P.q("qS")
        if ti == 0 or DEBUG or STAGE < 8:
            P.waitonly("v", [("qL", xld)])
            for g_ in range(NTG):
                ln_accum(xt[g_][:, :], g_)
        ln_chain4()
        v_op(lambda h: h.scalar_tensor_tensor(
            stat[:, 36:40], stat[:, 8:12], -1.0, stat[:, 20:24],
            OP.mult, OP.mult))
        P.waitonly("v", [("qL", xld)])
        for g_ in range(NTG):
            if g_ > 0:
                P.waitonly("v", [("qX", P.q("qX"))])
            v_op(lambda h, x=xt[g_], g_=g_: h.tensor_scalar(
                tmp_tm[:, :], x[:, :], stat[:, 20 + g_:21 + g_],
                stat[:, 36 + g_:37 + g_], OP.mult, OP.add))
            P.dmat(lnTA[:].rearrange("p (i t) -> p i t", i=8)[
                       :, :, g_ * 128:(g_ + 1) * 128],
                   tmp_tm[:].rearrange("p (i k) -> p i k", i=8),
                   waits=[("v", P.n("v"))])
        lnT_ready = P.q("qX")

        for pj in range(3):
            P.enter_phase()
            if pj > 0:
                wn = ("WqT", "WkT", "WvT")[pj]
                for r in range(8):
                    P.dma("g", "qS", wbuf[r][:, :],
                          Wins[wn][r * 128:(r + 1) * 128, :],
                          waits=[("t", P.n("t"))])
            wrdy = P.q("qS") if pj > 0 else wq_ready
            P.waitonly("t", [("qS", wrdy), ("qX", lnT_ready)])
            evcv = {}
            for oc in range(8):
                slot = (oc % 2) * 512
                if oc >= 2:
                    P.waitonly("t", [("s", evcv[oc - 2])])
                for ic in range(8):
                    P.mm(lambda h, oc=oc, ic=ic, slot=slot: h.matmul(
                        psA[:, slot:slot + 512],
                        wbuf[ic][:, oc * 128:(oc + 1) * 128],
                        lnT[ic][:, :], start=(ic == 0), stop=(ic == 7)),
                        inc=(ic == 7))
                ps = psA[:, slot:slot + 512]
                if pj == 0:
                    s_op(lambda h, o=q_fm[oc], ps=ps: h.copy(o[:, :], ps),
                         waits=[("t", P.n("t"))])
                elif pj == 1:
                    s_op(lambda h, o=kev[oc], ps=ps: h.copy(
                        o[0:64, :], ps[0:64, :]), waits=[("t", P.n("t"))])
                    s_op(lambda h, o=kod[oc], ps=ps: h.copy(
                        o[64:128, :], ps[64:128, :]))
                else:
                    s_op(lambda h, o=v_fm[oc], ps=ps: h.copy(o[:, :], ps),
                         waits=[("t", P.n("t"))])
                evcv[oc] = P.n("s")
        if DEBUG == 2 and ti == 0:
            for c in range(8):
                P.dma("g", "qS", y_out[c * 128:(c + 1) * 128, 0:512],
                      q_fm[c][:, :], waits=[("v", P.n("v"))])
                P.dma("g", "qS", y_out[1024 + c * 128:1024 + (c + 1) * 128, 0:512],
                      kev[c][:, :])
                P.dma("g", "qS", y_out[2048 + c * 128:2048 + (c + 1) * 128, 0:512],
                      kod[c][:, :])
                P.dma("g", "qS", y_out[3072 + c * 128:3072 + (c + 1) * 128, 0:512],
                      v_fm[c][:, :])
        for c in range(8):
            P.dmat(vTe[c][:].rearrange("p (w k) -> p w k", k=80)[:, :, 0:64],
                   v_fm[c][0:64, :].rearrange("p (w k) -> p w k", w=NTG),
                   waits=[("s", P.n("s"))] if c == 0 else None)
            P.dmat(vTo[c][:].rearrange("p (w k) -> p w k", k=80)[:, :, 0:64],
                   v_fm[c][64:128, :].rearrange("p (w k) -> p w k", w=NTG))
        vT_ready = P.q("qX")
        for r in range(8):
            P.dma("g", "qS", wbuf[r][:, :], Wins["WoT"][r * 128:(r + 1) * 128, :],
                  waits=[("t", P.n("t"))])
        wo_ready = P.q("qS")

        if STAGE >= 4:
            P.enter_phase()
        evac_prev = None
        av_prev = None
        for w_ in (range(NTG) if STAGE >= 4 else []):
            if w_ == 0:
                P.waitonly("t", [("v", P.n("v")), ("s", P.n("s"))])
            else:
                P.waitonly("t", [("v", evac_prev)])
            for s_ in range(16):
                c, p_ = s_ // 2, s_ % 2
                kj = (kev if p_ == 0 else kod)[c]
                kg = kg_ev if p_ == 0 else kg_od
                P.mm(lambda h, q=q_fm[c], kg=kg, c=c, s_=s_, w_=w_: h.matmul(
                    psA[:, 256 * s_:256 * s_ + G],
                    q[:, w_ * 128:(w_ + 1) * 128],
                    kg[:, c * G:(c + 1) * G], start=True, stop=True))
                P.mm(lambda h, q=q_fm[c], kj=kj, s_=s_, w_=w_: h.matmul(
                    psA[:, 256 * s_ + 64:256 * s_ + 192],
                    q[:, w_ * 128:(w_ + 1) * 128],
                    kj[:, w_ * 128:(w_ + 1) * 128],
                    start=True, stop=True), inc=(s_ == 15))
            sc_done = P.n("t")
            if DEBUG == 6 and ti == 0 and w_ == 0:
                v_op(lambda h: h.tensor_copy(hbuf[0][:, :], psA[:, 0:4096]),
                     waits=[("t", sc_done)], big=True)
                for qq in range(4):
                    P.dma("g", "qS", y_out[qq * 128:(qq + 1) * 128, :],
                          hbuf[0][:, qq * 1024:(qq + 1) * 1024],
                          waits=[("v", P.n("v"))])
                P.waitonly("s", [("qS", P.q("qS"))])
                P.waitonly("t", [("qS", P.q("qS"))])
                P.waitonly("v", [("qS", P.q("qS"))])
            s_op(lambda h: h.activation(
                pgc[:, :],
                psA[:].rearrange("p (s k) -> p s k", k=256)[:, :, 0:64],
                AF.Exp, bias=0.0, scale=0.125), waits=[("t", sc_done)])
            v_op(lambda h: h.tensor_copy(
                probs_g[:].rearrange("p (s k) -> p s k", k=128)[:, :, 0:64],
                pgc[:].rearrange("p (s k) -> p s k", k=64)),
                 waits=[("s", P.n("s"))], big=True)
            s_op(lambda h: h.activation(
                probs_l[:, :],
                psA[:].rearrange("p (s k) -> p s k", k=256)[:, :, 64:192],
                AF.Exp, bias=0.0, scale=0.125))
            exp_done = P.n("s")
            P.dmat(probsT_g.rearrange("p (s k) -> p s k", s=16),
                   probs_g[:].rearrange("p (s k) -> p s k", s=16),
                   waits=([("v", P.n("v"))] +
                          ([("t", av_prev)] if w_ > 0 else [])))
            trg = P.q("qX")
            P.dmat(probsT_l.rearrange("p (s k) -> p s k", s=16),
                   probs_l[:].rearrange("p (s k) -> p s k", s=16),
                   waits=[("s", P.n("s"))])
            trl = P.q("qX")
            tr_done = trl
            if DEBUG == 7 and ti == 0 and w_ == 0:
                P.waitonly("g", [("qX", tr_done), ("s", exp_done)])
                P.dma("g", "qS", y_out[0:128, 0:1024], probsT_l[:, 0:1024])
                P.dma("g", "qS", y_out[128:256, 0:1024], probsT_l[:, 1024:2048])
                P.dma("g", "qS", y_out[256:384, 0:1024], probsT_g[:, 0:1024])
                P.dma("g", "qS", y_out[384:512, 0:1024], probsT_g[:, 1024:2048])
                P.dma("g", "qS", y_out[512:640, 0:1024], probs_g[:, 0:1024])
                P.dma("g", "qS", y_out[640:768, 0:1024], probs_g[:, 1024:2048])
                P.waitonly("s", [("qS", P.q("qS"))])
                P.waitonly("t", [("qS", P.q("qS"))])
                P.waitonly("v", [("qS", P.q("qS"))])
            P.waitonly("t", [("qX", trl), ("s", exp_done)])
            for s_ in range(16):
                c, p_ = s_ // 2, s_ % 2
                vv = (vTe if p_ == 0 else vTo)[c]
                P.mm(lambda h, s_=s_, vv=vv, w_=w_: h.matmul(
                    psA[:, 256 * s_:256 * s_ + 65],
                    probsT_l[:, s_ * 128:(s_ + 1) * 128],
                    vv[:, w_ * 80:w_ * 80 + 65], start=True, stop=False))
                P.mm(lambda h, s_=s_, c=c, p_=p_: h.matmul(
                    psA[:, 256 * s_:256 * s_ + 65],
                    probsT_g[0:64, s_ * 128:(s_ + 1) * 128],
                    vgT[:, c * 160 + p_ * 80:c * 160 + p_ * 80 + 65],
                    start=False, stop=True), inc=(s_ == 15))
            av_done = P.n("t")
            v_op(lambda h: h.tensor_copy(
                attn_un[:].rearrange("p (s k) -> p s k", k=65),
                psA[:].rearrange("p (s k) -> p s k", k=256)[:, :, 0:65]),
                waits=[("t", av_done)], big=True)
            evac_prev = P.n("v")
            av_prev = av_done
            v_op(lambda h: h.reciprocal(
                srec[:].rearrange("p (s k) -> p s k", k=1),
                attn_un[:].rearrange("p (s k) -> p s k", k=65)[:, :, 64:65]))
            v_op(lambda h, o=attn_tm[w_]: h.tensor_tensor(
                o[:].rearrange("p (s k) -> p s k", k=64),
                attn_un[:].rearrange("p (s k) -> p s k", k=65)[:, :, 0:64],
                srec[:, :, None].broadcast_to([128, 16, 64]), OP.mult))
            if DEBUG == 4 and ti == 0 and w_ == 0:
                P.dma("g", "qS", y_out[0:128, :], probs_l[:, 0:1024],
                      waits=[("v", P.n("v"))])
                P.dma("g", "qS", y_out[128:256, :], probs_l[:, 1024:2048])
                P.dma("g", "qS", y_out[256:384, :], probs_g[:, 0:1024])
                P.dma("g", "qS", y_out[384:512, :], probs_g[:, 1024:2048])
                P.dma("g", "qS", y_out[512:640, :], attn_un[:, 0:1024])
                P.dma("g", "qS", y_out[640:768, 0:16], srec[:, :])
                P.waitonly("s", [("qS", P.q("qS"))])
                P.waitonly("t", [("qS", P.q("qS"))])
                P.waitonly("v", [("qS", P.q("qS"))])

        if STAGE < 5:
            break
        if DEBUG == 3 and ti == 0:
            for g_ in range(NTG):
                P.dma("g", "qS", y_out[g_ * 128:(g_ + 1) * 128, :],
                      attn_tm[g_][:, :], waits=[("v", P.n("v"))])

        P.enter_phase()
        for g_ in range(NTG):
            P.dmat(parena[:, 0:DFF].rearrange("p (i t) -> p i t", i=8)[
                       :, :, g_ * 128:(g_ + 1) * 128],
                   attn_tm[g_][:].rearrange("p (i k) -> p i k", i=8),
                   waits=[("v", P.n("v"))])
        afm_ready = P.q("qX")
        P.waitonly("t", [("qS", wo_ready), ("qX", afm_ready)])
        evcv = {}
        for oc in range(8):
            slot = (oc % 2) * 512
            if oc >= 2:
                P.waitonly("t", [("s", evcv[oc - 2])])
            for ic in range(8):
                P.mm(lambda h, oc=oc, ic=ic, slot=slot: h.matmul(
                    psA[:, slot:slot + 512],
                    wbuf[ic][:, oc * 128:(oc + 1) * 128],
                    afmT[ic][:, :], start=(ic == 0), stop=(ic == 7)),
                    inc=(ic == 7))
            s_op(lambda h, oc=oc, slot=slot: h.copy(
                devac[oc][:, :], psA[:, slot:slot + 512]),
                 waits=[("t", P.n("t"))])
            evcv[oc] = P.n("s")
        for g_ in range(NTG):
            for oc in range(8):
                P.mm(lambda h, g_=g_, oc=oc: h.transpose(
                    psA[:, 2048 + oc * 128:2176 + oc * 128],
                    devac[oc][:, g_ * 128:(g_ + 1) * 128], idf_sb[:]),
                    waits=([("s", P.n("s")), ("v", P.n("v"))]
                           if (g_ == 0 and oc == 0) else
                           ([("v", P.n("v"))] if oc == 0 else None)),
                    inc=(oc == 7))
            v_op(lambda h, g_=g_: h.tensor_tensor(
                xt[g_][:, :], xt[g_][:, :], psA[:, 2048:3072], OP.add),
                 waits=[("t", P.n("t"))], big=True)
            ln_accum(xt[g_][:, :], g_)
            v_op(lambda h, x=xt[g_], o=stat[:, 12 + g_:13 + g_]:
                 h.tensor_reduce(o, x[:, :], AX.X, OP.max))
            v_op(lambda h, x=xt[g_], o=stat[:, 16 + g_:17 + g_]:
                 h.tensor_reduce(o, x[:, :], AX.X, OP.min))

        if DEBUG == 1:
            for g_ in range(NTG):
                P.dma("g", "qS", y_out[t0 + g_ * 128:t0 + (g_ + 1) * 128, :],
                      xt[g_][:, :], waits=[("v", P.n("v"))])

        P.enter_phase()
        for je in range(2):
            gq = ("gA", "gB")[je]
            P.dma("g", gq,
                  ternG_v[je].rearrange("p (i k) -> p i k", i=8),
                  S_tern[0].rearrange("(i p) (e k) -> p i e k",
                                      p=128, e=8)[:, :, je],
                  waits=[("t", P.n("t"))] if je == 0 else None)
            P.dma("g", gq,
                  ternU_v[je].rearrange("p (i k) -> p i k", i=8),
                  S_tern[1].rearrange("(i p) (e k) -> p i e k",
                                      p=128, e=8)[:, :, je])
        C = lambda a, b: stat[:, a:b]
        v_op(lambda h: h.tensor_copy(C(44, 48), C(16, 20)))
        v_op(lambda h: h.tensor_copy(C(40, 44), C(12, 16)))
        ln_chain4()
        v_op(lambda h: h.tensor_tensor(C(12, 16), C(40, 44), C(8, 12),
                                       OP.subtract))
        v_op(lambda h: h.tensor_tensor(C(16, 20), C(8, 12), C(44, 48),
                                       OP.subtract))
        v_op(lambda h: h.tensor_tensor(C(12, 16), C(12, 16), C(16, 20),
                                       OP.max))
        v_op(lambda h: h.tensor_tensor(C(12, 16), C(12, 16), C(20, 24),
                                       OP.mult))
        v_op(lambda h: h.tensor_scalar(C(12, 16), C(12, 16), 1e-5, None,
                                       OP.max))
        v_op(lambda h: h.reciprocal(C(16, 20), C(12, 16)))
        v_op(lambda h: h.tensor_scalar(C(16, 20), C(16, 20), 127.0, None,
                                       OP.mult))
        v_op(lambda h: h.tensor_scalar(C(12, 16), C(12, 16), 1.0 / 127.0,
                                       None, OP.mult))
        v_op(lambda h: h.tensor_tensor(C(36, 40), C(20, 24), C(16, 20),
                                       OP.mult))
        v_op(lambda h: h.scalar_tensor_tensor(C(40, 44), C(8, 12), -1.0,
                                              C(36, 40), OP.mult, OP.mult))
        v_op(lambda h: h.tensor_scalar(
            fstat[:].rearrange("p (g k) -> p g k", k=4)[:, :, 0:1],
            C(12, 16).rearrange("p (g k) -> p g k", k=1), fw_sb[:, 0:1],
            None, OP.mult))
        v_op(lambda h: h.tensor_scalar(
            fstat[:].rearrange("p (g k) -> p g k", k=4)[:, :, 1:2],
            C(12, 16).rearrange("p (g k) -> p g k", k=1), fw_sb[:, 1:2],
            None, OP.mult))
        xq2_cnt = {}
        for g_ in range(NTG):
            v_op(lambda h, x=xt[g_], g_=g_: h.tensor_scalar(
                scr2[:, :], x[:, :], stat[:, 36 + g_:37 + g_],
                stat[:, 40 + g_:41 + g_], OP.mult, OP.add))
            v_op(lambda h: h.tensor_scalar(scr2[:, :], scr2[:, :], C32, C32,
                                           OP.add, OP.subtract))
            v_op(lambda h: h.tensor_scalar(scr[:, :], scr2[:, :], C64, C64,
                                           OP.add, OP.subtract))
            v_op(lambda h: h.scalar_tensor_tensor(
                tmp_tm[:, :], scr2[:, :], omw_sb[:, 0:1], scr[:, :],
                OP.mult, OP.add),
                 waits=[("qX", P.q("qX"))] if g_ > 0 else None)
            P.dmat(lnTA[:].rearrange("p (i t) -> p i t", i=8)[
                       :, :, g_ * 128:(g_ + 1) * 128],
                   tmp_tm[:].rearrange("p (i k) -> p i k", i=8),
                   waits=[("v", P.n("v"))])
            xq2_cnt[g_] = P.q("qX")
        xq2_ready = P.q("qX")

        if STAGE < 6:
            break
        P.enter_phase()
        je_cons = {}
        slot_done = {}
        for je in range(8):
            par = je % 2
            gq = ("gA", "gB")[par]
            if je >= 2:
                P.waitonly("g", [("t", je_cons[je - 2])])
                P.dma("g", gq,
                      ternG_v[par].rearrange("p (i k) -> p i k", i=8),
                      S_tern[0].rearrange("(i p) (e k) -> p i e k",
                                          p=128, e=8)[:, :, je])
                P.dma("g", gq,
                      ternU_v[par].rearrange("p (i k) -> p i k", i=8),
                      S_tern[1].rearrange("(i p) (e k) -> p i e k",
                                          p=128, e=8)[:, :, je])
            P.waitonly("t", [(gq, P.q(gq)),
                             ("qX", xq2_cnt[0] if je == 0 else xq2_ready)])
            for g_ in range(NTG):
                gslot = (g_ % 2) * 1024
                uslot = gslot + 512
                if je == 0 and g_ > 0:
                    P.waitonly("t", [("qX", xq2_cnt[g_])])
                if (g_ % 2) in slot_done:
                    P.waitonly("t", [("v", slot_done[g_ % 2][0]),
                                     ("s", slot_done[g_ % 2][1])])
                for ic in range(8):
                    P.mm(lambda h, g_=g_, ic=ic, par=par, gslot=gslot: h.matmul(
                        psA[:, gslot:gslot + 512],
                        lnT[ic][:, g_ * 128:(g_ + 1) * 128],
                        ternG_v[par][:, ic * 512:(ic + 1) * 512],
                        start=(ic == 0), stop=(ic == 7)))
                    P.mm(lambda h, g_=g_, ic=ic, par=par, uslot=uslot: h.matmul(
                        psA[:, uslot:uslot + 512],
                        lnT[ic][:, g_ * 128:(g_ + 1) * 128],
                        ternU_v[par][:, ic * 512:(ic + 1) * 512],
                        start=(ic == 0), stop=(ic == 7)), inc=(ic == 7))
                s_op(lambda h, g_=g_, gslot=gslot: h.activation(
                    scr[:, 0:512], psA[:, gslot:gslot + 512], AF.Silu,
                    bias=0.0, scale=fstat[:, g_ * 4:g_ * 4 + 1]),
                     waits=[("t", P.n("t")), ("v", P.n("v"))])
                v_op(lambda h, g_=g_, je=je, uslot=uslot: h.tensor_tensor(
                    hbuf[g_][:, je * 512:(je + 1) * 512], scr[:, 0:512],
                    psA[:, uslot:uslot + 512], OP.mult),
                     waits=[("s", P.n("s"))], big=True)
                slot_done[g_ % 2] = (P.n("v"), P.n("s"))
                if je == 0:
                    v_op(lambda h, g_=g_: h.tensor_reduce(
                        stat[:, 24 + g_:25 + g_], hbuf[g_][:, 0:512],
                        AX.X, OP.max, apply_absolute_value=True))
                else:
                    v_op(lambda h, g_=g_, je=je: h.tensor_reduce(
                        stat[:, 28:29], hbuf[g_][:, je * 512:(je + 1) * 512],
                        AX.X, OP.max, apply_absolute_value=True))
                    v_op(lambda h, g_=g_: h.tensor_tensor(
                        stat[:, 24 + g_:25 + g_], stat[:, 24 + g_:25 + g_],
                        stat[:, 28:29], OP.max))
            je_cons[je] = P.n("t")

        P.enter_phase()
        for j2 in range(2):
            P.dma("g", ("wA", "wB")[j2], wdb[:, j2 * D:(j2 + 1) * D],
                  S_tern[2][j2 * 128:(j2 + 1) * 128, :])
        kapr = fstat[:].rearrange("p (g k) -> p g k", k=4)
        for h_ in range(2):
            c0 = 12 + 2 * h_
            s0 = 16 + 2 * h_
            v_op(lambda h, c0=c0, h_=h_: h.tensor_tensor(
                stat[:, c0:c0 + 2].rearrange("p (g k) -> p g k", k=1),
                stat[:, 24 + 2 * h_:26 + 2 * h_].rearrange(
                    "p (g k) -> p g k", k=1),
                kapr[:, 2 * h_:2 * h_ + 2, 1:2], OP.mult))
            v_op(lambda h, c0=c0: h.tensor_scalar(
                stat[:, c0:c0 + 2], stat[:, c0:c0 + 2], 1e-5, None, OP.max))
            v_op(lambda h, s0=s0, c0=c0: h.reciprocal(
                stat[:, s0:s0 + 2], stat[:, c0:c0 + 2]))
            v_op(lambda h, s0=s0, h_=h_: h.tensor_tensor(
                stat[:, s0:s0 + 2].rearrange("p (g k) -> p g k", k=1),
                stat[:, s0:s0 + 2].rearrange("p (g k) -> p g k", k=1),
                kapr[:, 2 * h_:2 * h_ + 2, 1:2], OP.mult))
            v_op(lambda h, s0=s0: h.tensor_scalar(
                stat[:, s0:s0 + 2], stat[:, s0:s0 + 2], 127.0, None,
                OP.mult))
            v_op(lambda h, h_=h_, c0=c0: h.tensor_scalar(
                kapr[:, 2 * h_:2 * h_ + 2, 2:3],
                stat[:, c0:c0 + 2].rearrange("p (g k) -> p g k", k=1),
                fw_sb[:, 2:3], 1.0 / 127.0, OP.mult, OP.mult))
            for g_ in (2 * h_, 2 * h_ + 1):
                v_op(lambda h, b=hbuf[g_], g_=g_: h.tensor_scalar(
                    b[:, :], b[:, :], stat[:, 16 + g_:17 + g_], C32,
                    OP.mult, OP.add))
                v_op(lambda h, b=hbuf[g_]: h.tensor_scalar(
                    b[:, :], b[:, :], C32, None, OP.subtract))
                v_op(lambda h, b=hbuf[g_]: h.tensor_scalar(
                    rsb[:, :], b[:, :], C64, C64, OP.add, OP.subtract))
                v_op(lambda h, b=hbuf[g_]: h.scalar_tensor_tensor(
                    b[:, :], b[:, :], omw_sb[:, 0:1], rsb[:, :],
                    OP.mult, OP.add))
                P.dma("g", "qS", S_hq[g_ * 128:(g_ + 1) * 128, :],
                      hbuf[g_][:, :], waits=[("v", P.n("v"))])
        hq_done = P.q("qS")

        P.enter_phase()
        mmcnt = {}
        qr8 = ("hA", "hB", "sA", "sB", "pA", "pB", "gA", "gB")
        for jc in range(32):
            par = jc % 2
            sl = (jc % 8) * TT
            hq_q = qr8[jc % 8]
            wd_q = ("wA", "wB")[par]
            if jc >= 8:
                P.waitonly("y", [("t", mmcnt[jc - 8])])
            if jc >= 2:
                P.waitonly("g", [("t", mmcnt[jc - 2])])
            P.dmat(lnTA[:, sl:sl + TT],
                   S_hq[0:TT, jc * 128:(jc + 1) * 128],
                   waits=[("qS", hq_done)] if jc < 8 else None, qn=hq_q)
            if jc >= 2:
                P.dma("g", wd_q, wdb[:, par * D:(par + 1) * D],
                      S_tern[2][jc * 128:(jc + 1) * 128, :])
            P.waitonly("t", [(hq_q, P.q(hq_q)), (wd_q, P.q(wd_q))])
            for g_ in range(NTG):
                for nh in range(2):
                    P.mm(lambda h, jc=jc, g_=g_, nh=nh, sl=sl, par=par:
                         h.matmul(
                        psA[:, g_ * 1024 + nh * 512:g_ * 1024 + nh * 512 + 512],
                        lnTA[:, sl + g_ * 128:sl + (g_ + 1) * 128],
                        wdb[:, par * D + nh * 512:par * D + (nh + 1) * 512],
                        start=(jc == 0), stop=(jc == 31)),
                        inc=(g_ == 3 and nh == 1))
            mmcnt[jc] = P.n("t")
        dn_done = P.n("t")
        if ti + 1 < NTILE and STAGE >= 8 and not DEBUG:
            t1 = t0 + TT
            nst = {}
            for g_ in range(NTG):
                pb = (scr, scr2)[g_ % 2]
                pq = ("qL", "qX")[g_ % 2]
                if g_ >= 2:
                    P.waitonly("y", [("v", nst[g_ - 2])])
                P.dma("y", pq, pb[:, :],
                      x_in[t1 + g_ * 128:t1 + (g_ + 1) * 128, :])
                v_op(lambda h, x=pb[:, :], g_=g_: h.tensor_scalar(
                    hbuf[0][:, 0:D], x, 1.0, 0.0, OP.mult, OP.add,
                    accum_out=stat[:, g_:g_ + 1]),
                     waits=([(pq, P.q(pq))] +
                            ([("qS", hq_done)] if g_ == 0 else [])),
                     big=True)
                v_op(lambda h, x=pb[:, :], g_=g_: h.scalar_tensor_tensor(
                    hbuf[0][:, D:2 * D], x, 1.0, x, OP.mult, OP.mult,
                    accum_out=stat[:, 4 + g_:5 + g_]), big=True)
                nst[g_] = P.n("v")
        for g_ in range(NTG):
            v_op(lambda h, g_=g_: h.scalar_tensor_tensor(
                xt[g_][:, :], psA[:, g_ * 1024:(g_ + 1) * 1024],
                fstat[:, g_ * 4 + 2:g_ * 4 + 3],
                xt[g_][:, :], OP.mult, OP.add),
                 waits=[("t", dn_done)] if g_ == 0 else None, big=True)
            if not DEBUG:
                P.dma("g", "qS", y_out[t0 + g_ * 128:t0 + (g_ + 1) * 128, :],
                      xt[g_][:, :], waits=[("v", P.n("v"))])
            elif DEBUG == 2:
                pass

    if STAGE < 7:
        P.enter_phase()
        for g_ in range(NTG):
            P.dma("g", "qS", y_out[g_ * 128:(g_ + 1) * 128, :], scr[:, :],
                  waits=[("v", P.n("v")), ("s", P.n("s")), ("t", P.n("t"))])

    P.enter_phase()
    P.waitonly("g", [(qn, P.q(qn)) for qn in P.QUEUES if P.q(qn) > 0])

    P.finalize()
    with es, nc.Block() as block:
        block.vector(lambda h: P.replay("v", h))
        block.scalar(lambda h: P.replay("s", h))
        block.tensor(lambda h: P.replay("t", h))
        block.gpsimd(lambda h: P.replay("g", h))
        block.sync(lambda h: P.replay("y", h))
    return nc


_NC_CACHE = {}


def _get_nc():
    if "nc" not in _NC_CACHE:
        nc = bass.Bass("TRN2", target_bir_lowering=False,
                       dynamic_dma_scratch_size=16384)
        _NC_CACHE["nc"] = build(nc)
    return _NC_CACHE["nc"]


def kernel(L0_attn_x, L0_attn_Wq, L0_attn_Wk, L0_attn_Wv, L0_attn_Wo,
           L0_ffn_w_gate, L0_ffn_w_up, L0_ffn_w_down):
    x = np.ascontiguousarray(np.asarray(L0_attn_x, dtype=np.float32))
    B, T, _ = x.shape
    WT = {
        "WqT": np.ascontiguousarray(np.asarray(L0_attn_Wq, np.float32).T),
        "WkT": np.ascontiguousarray(np.asarray(L0_attn_Wk, np.float32).T),
        "WvT": np.ascontiguousarray(np.asarray(L0_attn_Wv, np.float32).T),
        "WoT": np.ascontiguousarray(np.asarray(L0_attn_Wo, np.float32).T),
        "WgT": np.ascontiguousarray(np.asarray(L0_ffn_w_gate, np.float32).T),
        "WuT": np.ascontiguousarray(np.asarray(L0_ffn_w_up, np.float32).T),
        "WdT": np.ascontiguousarray(np.asarray(L0_ffn_w_down, np.float32).T),
    }
    cst = np.zeros((128, 8), np.float32)
    cst[:, 0] = np.float32(2.0**-30)
    cst[:, 1] = 1.0
    idf = np.eye(128, dtype=np.float32)

    nc = _get_nc()
    in_maps = []
    for c in range(8):
        b, hf = c // 2, c % 2
        m = {"x": np.ascontiguousarray(x[b, hf * TCORE:(hf + 1) * TCORE, :]),
             "xg": np.ascontiguousarray(x[b, 0:G, :]),
             "cst": cst, "idf": idf}
        m.update(WT)
        in_maps.append(m)
    res = run_bass_kernel_spmd(nc, in_maps, core_ids=list(range(8)))
    out = np.empty((B, T, D), np.float32)
    for c in range(8):
        b, hf = c // 2, c % 2
        out[b, hf * TCORE:(hf + 1) * TCORE, :] = res.results[c]["y"]
    return out
